# version 09
# baseline (speedup 1.0000x reference)
import math
import contextlib
import numpy as np
import ml_dtypes

import concourse.bass as bass
import concourse.mybir as mybir
from concourse.bass_utils import run_bass_kernel_spmd
from concourse.alu_op_type import AluOpType

F32 = mybir.dt.float32
BF16 = mybir.dt.bfloat16
I32 = mybir.dt.int32
AF = mybir.ActivationFunctionType

B = 8
T = 2048
D = 2048
H = 16
KVH = 4
DH = 128
INTER = 5504
TK = 1152
NT = T // 128
ND = D // 128
NI = INTER // 128
NKT = TK // 128
EPS = 1e-5
QSCALE = 1.0 / math.sqrt(DH)
NEG = -1.0e9
DSLOTS = 6

VTILES = [0] + list(range(8, 16))
VSLOT = {t_: i for i, t_ in enumerate(VTILES)}


class Sem:
    def __init__(self, stack, nc, name):
        self.h = stack.enter_context(nc.semaphore(name))
        self.n = 0


class Prog:

    ENGINES = ("sync", "gpsimd", "vector", "scalar", "tensor")

    def __init__(self):
        self.th = {e: [] for e in self.ENGINES}

    def op(self, eng, waits, build, bump=None, k=1):
        ws = [(s.h, v) for (s, v) in
              (x for x in waits if x is not None)
              if s is not None and v is not None]
        tok = None
        hb = None
        if bump is not None:
            bump.n += k
            tok = bump.n
            hb = bump.h

        def th(e, ws=ws, build=build, hb=hb, k=k):
            for (sh, v) in ws:
                e.wait_ge(sh, v)
            inst = build(e)
            if hb is not None:
                inst.then_inc(hb, k)
        self.th[eng].append(th)
        return tok

    def dma(self, eng, q, out, in_, waits=()):
        return self.op(eng, waits,
                       lambda e, out=out, in_=in_: e.dma_start(out=out, in_=in_),
                       bump=q, k=16)

    def wait(self, eng, *waits):
        ws = [(s.h, v) for (s, v) in
              (x for x in waits if x is not None)
              if s is not None and v is not None]

        def th(e, ws=ws):
            for (sh, v) in ws:
                e.wait_ge(sh, v)
        self.th[eng].append(th)

    def finalize(self, blk):
        for name in self.ENGINES:
            thunks = self.th[name]
            if not thunks:
                continue
            dec = getattr(blk, name)

            @dec
            def _(e, thunks=thunks):
                for t in thunks:
                    t(e)


def mkap(base_ap, off, dims):
    return bass.AP(base_ap.tensor, base_ap.offset + off, [list(d) for d in dims])


PH2L = 4


def build_nc(phases=9, ph2l=None, taps=False, tbs=4, halves=2):
    if ph2l is None:
        ph2l = PH2L
    nc = bass.Bass("TRN2", target_bir_lowering=False)

    xT = nc.dram_tensor("xT", [D, T], F32, kind="ExternalInput").ap()
    wq = nc.dram_tensor("wq", [D, D], F32, kind="ExternalInput").ap()
    wk = nc.dram_tensor("wk", [D, KVH * DH], F32, kind="ExternalInput").ap()
    wv = nc.dram_tensor("wv", [D, KVH * DH], F32, kind="ExternalInput").ap()
    wo = nc.dram_tensor("wo", [D, D], F32, kind="ExternalInput").ap()
    gsh = nc.dram_tensor("gsh", [256, INTER], I32, kind="ExternalInput").ap()
    ush = nc.dram_tensor("ush", [256, INTER], I32, kind="ExternalInput").ap()
    dsh = nc.dram_tensor("dsh", [DSLOTS * 128, D], I32, kind="ExternalInput").ap()
    lutrep_h = nc.dram_tensor("lutrep", [128, 16], F32, kind="ExternalInput").ap()
    cosE_h = nc.dram_tensor("cosE", [128, T], BF16, kind="ExternalInput").ap()
    sinE_h = nc.dram_tensor("sinE", [128, T], BF16, kind="ExternalInput").ap()
    identb_h = nc.dram_tensor("identb", [128, 128], BF16, kind="ExternalInput").ap()
    identf_h = nc.dram_tensor("identf", [128, 128], F32, kind="ExternalInput").ap()
    permP_h = nc.dram_tensor("permP", [128, 128], BF16, kind="ExternalInput").ap()
    maskt_h = nc.dram_tensor("maskt", [128, 128], F32, kind="ExternalInput").ap()
    out_h = nc.dram_tensor("out", [D, T], F32, kind="ExternalOutput").ap()
    if taps:
        qav_o = nc.dram_tensor("qav_o", [128, H * T], BF16, kind="ExternalOutput").ap()
        kTr_o = nc.dram_tensor("kTr_o", [128, KVH * T], BF16, kind="ExternalOutput").ap()
        vtm_o = nc.dram_tensor("vtm_o", [128, 9 * 512], BF16, kind="ExternalOutput").ap()
        av_o = nc.dram_tensor("av_o", [128, H * T], BF16, kind="ExternalOutput").ap()
        x2_o = nc.dram_tensor("x2_o", [D, T], BF16, kind="ExternalOutput").ap()
        ln2_o = nc.dram_tensor("ln2_o", [1, 2 * T], F32, kind="ExternalOutput").ap()
        ln1_o = nc.dram_tensor("ln1_o", [128, ND * 512], BF16, kind="ExternalOutput").ap()
        aq_o = nc.dram_tensor("aq_o", [128, 4 * TK], F32, kind="ExternalOutput").ap()
        aT_o = nc.dram_tensor("aT_o", [128, NKT * 512], BF16, kind="ExternalOutput").ap()
        sums_o = nc.dram_tensor("sums_o", [128, 16], F32, kind="ExternalOutput").ap()
        rsum_o = nc.dram_tensor("rsum_o", [128, 8], F32, kind="ExternalOutput").ap()
        sc_o = nc.dram_tensor("sc_o", [128, 512], F32, kind="ExternalOutput").ap()
        ccg_o = nc.dram_tensor("ccg_o", [ND, NI, 128, 128], BF16, kind="ExternalOutput").ap()
        ccd_o = nc.dram_tensor("ccd_o", [8 * DSLOTS, ND, 128, 128], BF16, kind="ExternalOutput").ap()
        hb_o = nc.dram_tensor("hb_o", [128, NI * 1024], BF16, kind="ExternalOutput").ap()
        ln2b_o = nc.dram_tensor("ln2b_o", [128, ND * 1024], BF16, kind="ExternalOutput").ap()

    wq_bf = nc.dram_tensor("wq_bf", [ND, D, 128], BF16, kind="Internal").ap()
    wo_bf = nc.dram_tensor("wo_bf", [ND, D, 128], BF16, kind="Internal").ap()
    wk_img = nc.dram_tensor("wk_img", [128, KVH * ND * 128], BF16, kind="Internal").ap()
    wv_img = nc.dram_tensor("wv_img", [128, ND * KVH * DH], BF16, kind="Internal").ap()
    x2h = nc.dram_tensor("x2h", [D, T], BF16, kind="Internal").ap()
    ccg_in = nc.dram_tensor("ccg_in", [2, NI, 128, 128], BF16, kind="Internal").ap()
    ccu_in = nc.dram_tensor("ccu_in", [2, NI, 128, 128], BF16, kind="Internal").ap()
    ccd_in = nc.dram_tensor("ccd_in", [DSLOTS, ND, 128, 128], BF16, kind="Internal").ap()
    ccg = nc.dram_tensor("ccg", [ND, NI, 128, 128], BF16, kind="Internal",
                         addr_space="Shared").ap()
    ccu = nc.dram_tensor("ccu", [ND, NI, 128, 128], BF16, kind="Internal",
                         addr_space="Shared").ap()
    ccd = nc.dram_tensor("ccd", [8 * DSLOTS, ND, 128, 128], BF16, kind="Internal",
                         addr_space="Shared").ap()

    p = Prog()
    top = contextlib.ExitStack()
    with top:
        def _finish():
            p.wait("gpsimd", (gq, gq.n), (gst[0], gst[0].n), (gst[1], gst[1].n),
                   (gcv, gcv.n), (gmt, gmt.n))
            _b = top.enter_context(nc.Block())
            p.finalize(_b)
        qcst = Sem(top, nc, "qcst")
        qc = [Sem(top, nc, "qc0"), Sem(top, nc, "qc1")]
        qis = [Sem(top, nc, "qi0"), Sem(top, nc, "qi1")]
        qx = [Sem(top, nc, "qx0"), Sem(top, nc, "qx1")]
        qa = [Sem(top, nc, "qa0"), Sem(top, nc, "qa1")]
        qw = [Sem(top, nc, "qw0"), Sem(top, nc, "qw1")]
        qkv = Sem(top, nc, "qkv")
        qg = [Sem(top, nc, "qg0"), Sem(top, nc, "qg1")]
        qu = [Sem(top, nc, "qu0"), Sem(top, nc, "qu1")]
        qd = [Sem(top, nc, "qd0"), Sem(top, nc, "qd1")]
        gx = [Sem(top, nc, "gx0"), Sem(top, nc, "gx1")]
        gst = [Sem(top, nc, "gst0"), Sem(top, nc, "gst1")]
        gcv = Sem(top, nc, "gcv")
        gmt = Sem(top, nc, "gmt")
        gq = Sem(top, nc, "gq")
        pe = Sem(top, nc, "pe")
        dv = Sem(top, nc, "dv")
        ac = Sem(top, nc, "ac")
        pl = Sem(top, nc, "pl")
        cc = Sem(top, nc, "cc")

        identb = top.enter_context(nc.sbuf_tensor("identb_s", [128, 128], BF16))
        identf = top.enter_context(nc.sbuf_tensor("identf_s", [128, 128], F32))
        permP = top.enter_context(nc.sbuf_tensor("permP_s", [128, 128], BF16))
        maskt = top.enter_context(nc.sbuf_tensor("maskt_s", [128, 128], F32))
        luts = top.enter_context(nc.sbuf_tensor("luts", [128, 16], F32))
        lutsD = top.enter_context(nc.sbuf_tensor("lutsD", [128, 18], F32))
        onesb = top.enter_context(nc.sbuf_tensor("onesb", [128, 1], BF16))
        onesf = top.enter_context(nc.sbuf_tensor("onesf", [1, 128], F32))
        mu2r = top.enter_context(nc.sbuf_tensor("mu2r", [1, 2 * T], F32))
        scdbg = top.enter_context(nc.sbuf_tensor("scdbg", [128, 512], F32))
        ones1b = top.enter_context(nc.sbuf_tensor("ones1b", [1, 128], BF16))
        bsp = top.enter_context(nc.sbuf_tensor("bsp", [1, 3 * 512], BF16))
        tsp = top.enter_context(nc.sbuf_tensor("tsp", [1, 2 * 512], F32))

        ps = [top.enter_context(nc.psum_tensor(f"ps{i}", [128, 512], F32))
              for i in range(7)]
        free = {i: None for i in range(7)}

        phR = contextlib.ExitStack()
        cosE = phR.enter_context(nc.sbuf_tensor("cosE_s", [128, T], BF16))
        sinE = phR.enter_context(nc.sbuf_tensor("sinE_s", [128, T], BF16))

        ph0 = contextlib.ExitStack()
        wstage = ph0.enter_context(nc.sbuf_tensor("wstage", [128, 2 * D], F32))
        wstageb = ph0.enter_context(nc.sbuf_tensor("wstageb", [128, 2 * D], BF16))
        idxs = ph0.enter_context(nc.sbuf_tensor("idxs", [128, 2 * INTER], I32))
        idxbf = ph0.enter_context(nc.sbuf_tensor("idxbf", [128, INTER], BF16))
        macc = ph0.enter_context(nc.sbuf_tensor("macc", [128, 2 * INTER], BF16))
        mmp = ph0.enter_context(nc.sbuf_tensor("mmp", [128, INTER], BF16))

        for dst, src in [(cosE[:], cosE_h[:]), (sinE[:], sinE_h[:]),
                         (identb[:], identb_h[:]), (identf[:], identf_h[:]),
                         (permP[:], permP_h[:]), (maskt[:], maskt_h[:]),
                         (luts[:], lutrep_h[:])]:
            p.dma("sync", qcst, dst, src)
        consts_full = (qcst, qcst.n)
        p.op("vector", [consts_full],
             lambda e: e.tensor_reduce(lutsD[:, 16:17], luts[:],
                                       axis=mybir.AxisListType.X,
                                       op=AluOpType.add))
        p.op("vector", [], lambda e: e.memset(lutsD[:, 17:18], 0.0))
        p.op("vector", [],
             lambda e: e.tensor_scalar(lutsD[:, 16:17], lutsD[:, 16:17],
                                       1.0 / 16.0, None, op0=AluOpType.mult,
                                       op1=AluOpType.bypass))
        p.op("vector", [], lambda e: e.memset(lutsD[:, 17:18], 0.0))
        p.op("vector", [], lambda e: e.memset(lutsD[:, 17:18], 0.0))
        lut_prep = p.op("vector", [],
                        lambda e: e.tensor_scalar(
                            lutsD[:, 0:16], luts[:], lutsD[:, 16:17], None,
                            op0=AluOpType.subtract, op1=AluOpType.bypass),
                        bump=dv)
        p.op("gpsimd", [], lambda e: e.memset(onesf[:], 1.0), bump=pl)
        p.op("gpsimd", [], lambda e: e.memset(ones1b[:], 1.0), bump=pl)
        tok_ones = p.op("gpsimd", [], lambda e: e.memset(onesb[:], 1.0), bump=pl)

        conv_jobs = []
        for c in range(ND):
            conv_jobs.append((wq[128 * c:128 * (c + 1), :], D,
                              [((wq_bf, c), "blk")]))
        for c in range(ND):
            conv_jobs.append((wo[128 * c:128 * (c + 1), :], D,
                              [((wo_bf, c), "blk")]))
        for c in range(ND):
            dsts = [(wk_img[:, dk * 2048 + c * 128: dk * 2048 + c * 128 + 128],
                     ("col", dk)) for dk in range(KVH)]
            conv_jobs.append((wk[128 * c:128 * (c + 1), :], KVH * DH, dsts))
        for c in range(ND):
            conv_jobs.append((wv[128 * c:128 * (c + 1), :], KVH * DH,
                              [(wv_img[:, 512 * c:512 * (c + 1)], "direct")]))

        conv_act_tok = []
        conv_store_tok = []
        for j, (src, ncol, dsts) in enumerate(conv_jobs):
            half = (j % 2) * D
            par = j % 2
            w = [(ac, conv_act_tok[j - 2])] if j >= 2 else []
            p.dma("sync", qc[par], wstage[:, half:half + ncol], src, waits=w)
            w2 = [(qc[par], qc[par].n)]
            if j >= 2:
                w2.append((gcv, conv_store_tok[j - 2]))
            tok = p.op("scalar", w2,
                       lambda e, a=wstageb[:, half:half + ncol],
                       b=wstage[:, half:half + ncol]: e.copy(out=a, in_=b),
                       bump=ac)
            conv_act_tok.append(tok)
            for dst, kindd in dsts:
                if kindd == "blk":
                    base, cch = dst
                    out_ap = mkap(base, 16384 * cch,
                                  [(128, 128), (262144, ND), (1, 128)])
                    p.dma("gpsimd", gcv, out_ap, wstageb[:, half:half + ncol],
                          waits=[(ac, tok)])
                elif kindd == "direct":
                    p.dma("gpsimd", gcv, dst, wstageb[:, half:half + ncol],
                          waits=[(ac, tok)])
                else:
                    dk = kindd[1]
                    p.dma("gpsimd", gcv, dst,
                          wstageb[:, half + 128 * dk:half + 128 * (dk + 1)],
                          waits=[(ac, tok)])
            conv_store_tok.append(gcv.n)
        conv_stores_full = (gcv, gcv.n)
        conv_acts = ac.n

        ph0.close()
        ph0_guard = conv_stores_full

        _bc_last = [None]

        def bcast_row(row, bank, dve_waits, pe_waits):
            b1 = bsp[0:1, 0:512]
            b2 = bsp[0:1, 512:1024]
            b3 = bsp[0:1, 1024:1536]
            t1 = tsp[0:1, 0:512]
            t2 = tsp[0:1, 512:1024]
            w0 = list(dve_waits)
            if _bc_last[0] is not None:
                w0.append((pe, _bc_last[0]))
            p.op("vector", w0, lambda e: e.tensor_copy(b1, row))
            p.op("vector", [], lambda e: e.tensor_tensor(
                t1, row, b1, op=AluOpType.subtract))
            p.op("vector", [], lambda e: e.tensor_copy(b2, t1))
            p.op("vector", [], lambda e: e.tensor_tensor(
                t2, t1, b2, op=AluOpType.subtract))
            dtok = p.op("vector", [], lambda e: e.tensor_copy(b3, t2), bump=dv)
            p.op("tensor", [(dv, dtok)] + pe_waits,
                 lambda e: e.matmul(ps[bank][:], ones1b[:], b1,
                                    start=True, stop=False))
            p.op("tensor", [], lambda e: e.matmul(ps[bank][:], ones1b[:], b2,
                                                  start=False, stop=False))
            tok = p.op("tensor", [],
                       lambda e: e.matmul(ps[bank][:], ones1b[:], b3,
                                          start=False, stop=True), bump=pe)
            _bc_last[0] = tok
            return tok

        if phases <= 0:
            phR.close()
            p.dma("gpsimd", gq, out_h[0:128, 0:128], maskt[:])
            _finish()
            return nc

        ph1 = contextlib.ExitStack()
        ln1b = ph1.enter_context(nc.sbuf_tensor("ln1b", [128, ND * 512], BF16))
        xst = ph1.enter_context(nc.sbuf_tensor("xst", [128, 4 * 512], F32))
        xbfst = ph1.enter_context(nc.sbuf_tensor("xbfst", [128, 2 * 512], BF16))
        sqst = ph1.enter_context(nc.sbuf_tensor("sqst", [128, 2 * 512], BF16))
        wqb = ph1.enter_context(nc.sbuf_tensor("wqb", [128, 2 * 2048], BF16))
        wvr = ph1.enter_context(nc.sbuf_tensor("wvr", [128, ND * KVH * DH], BF16))
        wkr = ph1.enter_context(nc.sbuf_tensor("wkr", [128, KVH * ND * 128], BF16))
        qraw = ph1.enter_context(nc.sbuf_tensor("qraw", [128, 2 * 512], BF16))
        scrA = ph1.enter_context(nc.sbuf_tensor("scrA", [128, 2 * 512], BF16))
        scrB = ph1.enter_context(nc.sbuf_tensor("scrB", [128, 2 * 512], BF16))
        stat = ph1.enter_context(nc.sbuf_tensor("stat", [1, 4 * 512], F32))
        ph2hold = contextlib.ExitStack()
        qav = ph2hold.enter_context(
            nc.sbuf_tensor("qav", [128, H * T], BF16, side="right"))
        kTr = ph2hold.enter_context(
            nc.sbuf_tensor("kTr", [128, KVH * T], BF16, side="right"))
        vtm = ph2hold.enter_context(
            nc.sbuf_tensor("vtm", [128, len(VTILES) * 512], BF16, side="right"))

        p.dma("sync", qkv, wvr[:], wv_img[:], waits=[ph0_guard])
        p.dma("sync", qkv, wkr[:], wk_img[:])
        kv_res_full = (qkv, qkv.n)

        prev_stat_pe = None
        prev_apply_dv = None
        prev_swap_pe = None
        prev_rope_dv = None
        prev_vev_ac = None
        for tb in range(tbs):
            tcs = slice(512 * tb, 512 * (tb + 1))

            stat_mm_tok = []
            for c in range(ND):
                buf = c % 2
                if c >= 2:
                    w = [(pe, stat_mm_tok[c - 2])]
                elif tb == 0:
                    w = [ph0_guard]
                else:
                    w = [(pe, prev_stat_pe)]
                p.dma("sync", qx[buf], xst[:, 512 * buf:512 * (buf + 1)],
                      xT[128 * c:128 * (c + 1), tcs], waits=w)
                xx = xst[:, 512 * buf:512 * (buf + 1)]
                xb = xbfst[:, 512 * buf:512 * (buf + 1)]
                sb = sqst[:, 512 * buf:512 * (buf + 1)]
                w = [(qx[buf], qx[buf].n)]
                if c >= 2:
                    w.append((pe, stat_mm_tok[c - 2]))
                elif tb > 0:
                    w.append((pe, prev_stat_pe))
                p.op("vector", w, lambda e, xb=xb, xx=xx: e.tensor_copy(xb, xx))
                tok = p.op("vector", [],
                           lambda e, sb=sb, xb=xb: e.tensor_tensor(
                               sb, xb, xb, op=AluOpType.mult), bump=dv)
                w2 = [(dv, tok)]
                if c == 0:
                    w2 += [free[0], free[1], (pl, tok_ones)]
                p.op("tensor", w2,
                     lambda e, xb=xb, c=c: e.matmul(
                         ps[0][0:1, :], onesb[:], xb,
                         start=(c == 0), stop=(c == ND - 1)))
                tok = p.op("tensor", [],
                           lambda e, sb=sb, c=c: e.matmul(
                               ps[1][0:1, :], onesb[:], sb,
                               start=(c == 0), stop=(c == ND - 1)), bump=pe)
                stat_mm_tok.append(tok)
            prev_stat_pe = stat_mm_tok[-1]
            stats_pe = pe.n

            mu_s = stat[0:1, 0:512]
            m2_s = stat[0:1, 512:1024]
            sg_s = stat[0:1, 1024:1536]
            r_s = stat[0:1, 1536:2048]
            p.op("vector", [(pe, stats_pe)],
                 lambda e, mu_s=mu_s: e.tensor_scalar(
                     mu_s, ps[0][0:1, :], 1.0 / D, None,
                     op0=AluOpType.mult, op1=AluOpType.bypass))
            p.op("vector", [],
                 lambda e, m2_s=m2_s: e.tensor_scalar(
                     m2_s, ps[1][0:1, :], 1.0 / D, None,
                     op0=AluOpType.mult, op1=AluOpType.bypass))
            p.op("vector", [],
                 lambda e, sg_s=sg_s, mu_s=mu_s: e.tensor_tensor(
                     sg_s, mu_s, mu_s, op=AluOpType.mult))
            tok = p.op("vector", [],
                       lambda e, m2_s=m2_s, sg_s=sg_s: e.scalar_tensor_tensor(
                           m2_s, m2_s, EPS, sg_s,
                           op0=AluOpType.add, op1=AluOpType.subtract), bump=dv)
            free[0] = free[1] = (dv, tok)
            tok = p.op("scalar", [(dv, tok)],
                       lambda e, sg_s=sg_s, m2_s=m2_s: e.activation(
                           out=sg_s, in_=m2_s, func=AF.Sqrt, scale=1.0),
                       bump=ac)
            rtok = p.op("vector", [(ac, tok)],
                        lambda e, r_s=r_s, sg_s=sg_s: e.reciprocal(r_s, sg_s),
                        bump=dv)

            bcast_row(mu_s, 2, [(dv, rtok)],
                      [(pl, tok_ones), free[2], free[3]])
            bc_tok = bcast_row(r_s, 3, [], [])

            apply_tok = []
            for c in range(ND):
                buf = 2 + c % 2
                if c >= 2:
                    w = [(dv, apply_tok[c - 2])]
                elif tb > 0:
                    w = [(dv, prev_apply_dv)]
                else:
                    w = []
                p.dma("sync", qa[c % 2], xst[:, 512 * buf:512 * (buf + 1)],
                      xT[128 * c:128 * (c + 1), tcs], waits=w)
                xx = xst[:, 512 * buf:512 * (buf + 1)]
                w2 = [(qa[c % 2], qa[c % 2].n)]
                if c == 0:
                    w2.append((pe, bc_tok))
                p.op("vector", w2,
                     lambda e, xx=xx: e.tensor_tensor(xx, xx, ps[2][:],
                                                      op=AluOpType.subtract))
                tok = p.op("vector", [],
                           lambda e, xx=xx, dst=ln1b[:, 512 * c:512 * (c + 1)]:
                           e.tensor_tensor(dst, xx, ps[3][:], op=AluOpType.mult),
                           bump=dv)
                apply_tok.append(tok)
            prev_apply_dv = apply_tok[-1]
            apply_dv = dv.n
            free[2] = free[3] = (dv, apply_dv)

            qk_jobs = [("q", i) for i in range(ND)] + [("k", i) for i in range(KVH)]
            evac_tok = []
            rope_tok = []
            swap_tok = []
            wq_ld = []
            for j, (kind, di) in enumerate(qk_jobs):
                par = j % 2
                if kind == "q":
                    if j >= 2:
                        w = [(pe, swap_tok[j - 2])]
                    elif tb > 0:
                        w = [(pe, prev_swap_pe)]
                    else:
                        w = [conv_stores_full]
                    p.dma("sync", qw[par], wqb[:, 2048 * par:2048 * par + 1024],
                          mkap(wq_bf, di * 262144,
                               [(128, 128), (16384, 8), (1, 128)]), waits=w)
                    p.dma("sync", qw[par],
                          wqb[:, 2048 * par + 1024:2048 * par + 2048],
                          mkap(wq_bf, di * 262144 + 8 * 16384,
                               [(128, 128), (16384, 8), (1, 128)]))
                    wq_ld.append((qw[par], qw[par].n))
                else:
                    wq_ld.append(None)

                qps = ps[4 + par][:]
                w = [(dv, apply_dv)]
                if kind == "q":
                    w.append(wq_ld[j])
                else:
                    w.append(kv_res_full)
                if j >= 2:
                    w.append((ac, evac_tok[j - 2]))
                elif tb > 0:
                    w.append((ac, prev_vev_ac))
                for c in range(ND):
                    if kind == "q":
                        lhs = wqb[:, 2048 * par + 128 * c:2048 * par + 128 * (c + 1)]
                    else:
                        lhs = wkr[:, di * 2048 + 128 * c:di * 2048 + 128 * (c + 1)]
                    tok = p.op("tensor", w if c == 0 else [],
                               lambda e, lhs=lhs, qps=qps, c=c,
                               rhs=ln1b[:, 512 * c:512 * (c + 1)]: e.matmul(
                                   qps, lhs, rhs,
                                   start=(c == 0), stop=(c == ND - 1)),
                               bump=(pe if c == ND - 1 else None))
                mm_tok = tok
                qr = qraw[:, 512 * par:512 * (par + 1)]
                w = [(pe, mm_tok)]
                if j >= 2:
                    w += [(pe, swap_tok[j - 2]), (dv, rope_tok[j - 2])]
                elif tb > 0:
                    w += [(pe, prev_swap_pe), (dv, prev_rope_dv)]
                tok = p.op("scalar", w,
                           lambda e, qr=qr, qps=qps,
                           s=(QSCALE if kind == "q" else 1.0):
                           e.activation(out=qr, in_=qps, func=AF.Copy, scale=s),
                           bump=ac)
                evac_tok.append(tok)
                w = [(ac, tok)]
                if j >= 1:
                    w.append((dv, rope_tok[j - 1]))
                elif tb > 0:
                    w.append((dv, prev_rope_dv))
                stok = p.op("tensor", w,
                            lambda e, qr=qr: e.matmul(ps[6][:, 0:512], permP[:],
                                                      qr, start=True, stop=True),
                            bump=pe)
                swap_tok.append(stok)
                if kind == "q":
                    dst = qav[:, 2048 * di + 512 * tb:2048 * di + 512 * (tb + 1)]
                else:
                    dst = kTr[:, 2048 * di + 512 * tb:2048 * di + 512 * (tb + 1)]
                sA = scrA[:, 512 * par:512 * (par + 1)]
                sB = scrB[:, 512 * par:512 * (par + 1)]
                w = [(ac, tok)]
                if tb == 0 and j == 0:
                    w.append(ph0_guard)
                    w.append(consts_full)
                p.op("vector", w,
                     lambda e, sA=sA, qr=qr, cs=cosE[:, tcs]: e.tensor_tensor(
                         sA, qr, cs, op=AluOpType.mult))
                p.op("vector", [(pe, stok)],
                     lambda e, sB=sB, sn=sinE[:, tcs]: e.tensor_tensor(
                         sB, ps[6][:, 0:512], sn, op=AluOpType.mult))
                tok = p.op("vector", [],
                           lambda e, dst=dst, sA=sA, sB=sB: e.tensor_tensor(
                               dst, sA, sB, op=AluOpType.add), bump=dv)
                rope_tok.append(tok)
            prev_swap_pe = swap_tok[-1]
            prev_rope_dv = rope_tok[-1]
            free[4] = (ac, evac_tok[-2])
            free[5] = (ac, evac_tok[-1])

            vts = [t_ for t_ in VTILES if 512 * tb <= 128 * t_ < 512 * (tb + 1)]
            vev_tok = []
            for i, vt in enumerate(vts):
                par = i % 2
                vps = ps[4 + par][:]
                w = [(dv, apply_dv), kv_res_full]
                if i >= 2:
                    w.append((ac, vev_tok[i - 2]))
                else:
                    w.append(free[4 + par])
                col0 = 128 * vt - 512 * tb
                for c in range(ND):
                    lhs = ln1b[:, 512 * c + col0:512 * c + col0 + 128]
                    tok = p.op("tensor", w if c == 0 else [],
                               lambda e, lhs=lhs, vps=vps, c=c,
                               rhs=wvr[:, 512 * c:512 * (c + 1)]: e.matmul(
                                   vps, lhs, rhs,
                                   start=(c == 0), stop=(c == ND - 1)),
                               bump=(pe if c == ND - 1 else None))
                dst = vtm[:, 512 * VSLOT[vt]:512 * (VSLOT[vt] + 1)]
                w2 = [(pe, tok)]
                if tb == 0 and i == 0:
                    w2.append(ph0_guard)
                tok = p.op("scalar", w2,
                           lambda e, dst=dst, vps=vps: e.activation(
                               out=dst, in_=vps, func=AF.Copy, scale=1.0),
                           bump=ac)
                vev_tok.append(tok)
            for i in range(max(0, len(vts) - 2), len(vts)):
                free[4 + i % 2] = (ac, vev_tok[i])
            prev_vev_ac = vev_tok[-1] if vev_tok else prev_vev_ac

        ph1_pe = pe.n
        ph1_ac = ac.n
        ph1.close()
        phR.close()

        if taps:
            p.dma("gpsimd", gq, qav_o[:], qav[:],
                  waits=[(pe, pe.n), (dv, dv.n), (ac, ac.n)])
            p.dma("gpsimd", gq, kTr_o[:], kTr[:])
            p.dma("gpsimd", gq, vtm_o[:], vtm[:])
            p.dma("gpsimd", gq, ln1_o[:], ln1b[:])

        if phases <= 1:
            ph2hold.close()
            p.dma("gpsimd", gq, out_h[0:128, 0:128], maskt[:])
            _finish()
            return nc

        ph2 = contextlib.ExitStack()
        attnq = ph2.enter_context(nc.sbuf_tensor("attnq", [128, 4 * TK], F32))
        attnT = ph2.enter_context(nc.sbuf_tensor("attnT", [128, NKT * 512], BF16))
        sums = ph2.enter_context(nc.sbuf_tensor("sums", [128, 16], F32))
        rsum = ph2.enter_context(nc.sbuf_tensor("rsum", [128, 8], F32))
        midx = ph2.enter_context(nc.sbuf_tensor("midx", [128, 2 * 512], I32))
        mbf = ph2.enter_context(nc.sbuf_tensor("mbf", [128, 512], BF16))
        mpp = ph2.enter_context(nc.sbuf_tensor("mpp", [128, 512], BF16))
        mac2 = ph2.enter_context(nc.sbuf_tensor("mac2", [128, 2 * 512], BF16))

        mat_jobs = []
        for s_ in range(2):
            mat_jobs.append((gsh[128 * s_:128 * (s_ + 1), :], INTER, ccg_in[s_], True))
        for s_ in range(2):
            mat_jobs.append((ush[128 * s_:128 * (s_ + 1), :], INTER, ccu_in[s_], True))
        for s_ in range(DSLOTS):
            mat_jobs.append((dsh[128 * s_:128 * (s_ + 1), :], D, ccd_in[s_], False))
        mat_pieces = []
        for j, (srcj, wj, dstj, isgu) in enumerate(mat_jobs):
            for b0 in range(0, wj // 128, 4):
                b1 = min(wj // 128, b0 + 4)
                mat_pieces.append((srcj, dstj, isgu, b0, b1))
        mat_state = {"i": 0, "dv": [], "st": []}
        ph1_dv = dv.n

        def emit_mat_piece():
            i = mat_state["i"]
            if i >= len(mat_pieces):
                return
            mat_state["i"] += 1
            srcj, dstj, isgu, b0, b1 = mat_pieces[i]
            ncol = 128 * (b1 - b0)
            par = i % 2
            w = []
            if i < 2:
                w += [(pe, ph1_pe), (ac, ph1_ac), (dv, ph1_dv)]
            else:
                w.append((dv, mat_state["dv"][i - 2]))
            p.dma("sync", qis[par], midx[:, 512 * par:512 * par + ncol],
                  srcj[:, 128 * b0:128 * b1], waits=w)
            ib = mbf[:, 0:ncol]
            acc = mac2[:, 512 * par:512 * par + ncol]
            ltab = lutsD if isgu else luts
            w2 = [(qis[par], qis[par].n), (dv, lut_prep)]
            if i >= 2:
                w2.append((gmt, mat_state["st"][i - 2]))
            p.op("vector", w2,
                 lambda e, ib=ib, b=midx[:, 512 * par:512 * par + ncol]:
                 e.tensor_copy(ib, b))
            p.op("vector", [],
                 lambda e, acc=acc, ib=ib, ltab=ltab: e.tensor_scalar(
                     acc, ib, 0.0, ltab[:, 0:1],
                     op0=AluOpType.is_equal, op1=AluOpType.mult))
            tok = None
            for t in range(1, 16):
                p.op("vector", [],
                     lambda e, ib=ib, t=t, mpv=mpp[:, 0:ncol], ltab=ltab:
                     e.tensor_scalar(mpv, ib, float(t), ltab[:, t:t + 1],
                                     op0=AluOpType.is_equal, op1=AluOpType.mult))
                tok = p.op("vector", [],
                           lambda e, acc=acc, mpv=mpp[:, 0:ncol]: e.tensor_tensor(
                               acc, acc, mpv, op=AluOpType.add),
                           bump=(dv if t == 15 else None))
            mat_state["dv"].append(tok)
            out_ap = mkap(dstj, 16384 * b0,
                          [(128, 128), (16384, b1 - b0), (1, 128)])
            p.dma("gpsimd", gmt, out_ap, acc, waits=[(dv, tok)])
            mat_state["st"].append(gmt.n)

        last_exp = {i: None for i in range(6)}
        last_trT = {0: None, 1: None}
        avfree = None
        prev_transposes = None
        prev_scale = {}
        first_ph2 = True
        for h in range(H):
            kv = h // 4
            for qb in range(4):
                qts = [4 * qb + i for i in range(4)]
                memset_tok = None
                if qb < 2:
                    for i, qi in enumerate(qts):
                        kcols = TK if qi >= 9 else 128 * (qi + 1)
                        if kcols < TK:
                            w = []
                            if prev_transposes is not None:
                                w.append((pe, prev_transposes))
                            if first_ph2:
                                w.append((pe, ph1_pe))
                            memset_tok = p.op(
                                "vector", w,
                                lambda e, a=attnq[:, TK * i + kcols:TK * (i + 1)]:
                                e.memset(a, 0.0), bump=dv)
                exp_last = {}
                for i, qi in enumerate(qts):
                    kcols = TK if qi >= 9 else 128 * (qi + 1)
                    nseg = (kcols + 511) // 512
                    bank0 = 3 * (i % 2)
                    lhs = qav[:, 2048 * h + 128 * qi:2048 * h + 128 * (qi + 1)]
                    segs = []
                    for sg_ in range(nseg):
                        c0 = 512 * sg_
                        c1 = min(kcols, c0 + 512)
                        pieces = []
                        if c0 < 128:
                            pieces.append((c0, min(c1, 128), 2048 * kv + c0))
                        if c1 > 128:
                            w0 = max(c0, 128)
                            pieces.append((w0, c1, 2048 * kv + 1024 + (w0 - 128)))
                        bank = bank0 + sg_
                        w = []
                        if last_exp[bank] is not None:
                            w.append(last_exp[bank])
                        if first_ph2:
                            w.append((pe, ph1_pe))
                            w.append((ac, ph1_ac))
                        first = True
                        for (a0, a1, kc) in pieces:
                            tok = p.op("tensor", w if first else [],
                                       lambda e, bank=bank, a0=a0, a1=a1, kc=kc,
                                       c0=c0, lhs=lhs: e.matmul(
                                           ps[bank][:, a0 - c0:a1 - c0], lhs,
                                           kTr[:, kc:kc + (a1 - a0)],
                                           start=True, stop=True),
                                       bump=pe)
                            first = False
                        segs.append((bank, c0, c1, tok))
                    mask_dv = None
                    if qi <= 8:
                        bank, c0, c1, mmtok = segs[-1]
                        off = (kcols - 128) - c0
                        mask_dv = p.op("vector", [(pe, mmtok)],
                                       lambda e, bank=bank, off=off:
                                       e.tensor_tensor(
                                           ps[bank][:, off:off + 128],
                                           ps[bank][:, off:off + 128],
                                           maskt[:], op=AluOpType.add), bump=dv)
                    nparts = 0
                    for (bank, c0, c1, mmtok) in segs:
                        w = [(pe, mmtok)]
                        if mask_dv is not None and bank == segs[-1][0]:
                            w.append((dv, mask_dv))
                        if memset_tok is not None:
                            w.append((dv, memset_tok))
                        if nparts == 0 and i in prev_scale:
                            w.append((dv, prev_scale[i]))
                        scol = 4 * i + nparts
                        tok = p.op("scalar", w,
                                   lambda e, bank=bank, c0=c0, c1=c1, i=i,
                                   scol=scol: e.activation(
                                       out=attnq[:, TK * i + c0:TK * i + c1],
                                       in_=ps[bank][:, 0:c1 - c0], func=AF.Exp,
                                       scale=1.0,
                                       accum_out=sums[:, scol:scol + 1]),
                                   bump=ac)
                        last_exp[bank] = (ac, tok)
                        nparts += 1
                    exp_last[i] = (tok, nparts)
                first_ph2 = False
                if ph2l < 2:
                    continue
                scale_tok = {}
                for i, qi in enumerate(qts):
                    tok, nparts = exp_last[i]
                    p.op("vector", [(ac, tok)],
                         lambda e, i=i, c0=4 * i, nc_=nparts: e.tensor_reduce(
                             rsum[:, i:i + 1], sums[:, c0:c0 + nc_],
                             axis=mybir.AxisListType.X, op=AluOpType.add))
                for i in range(4):
                    p.op("vector", [],
                         lambda e, i=i: e.reciprocal(rsum[:, 4 + i:5 + i],
                                                     rsum[:, i:i + 1]))
                for i, qi in enumerate(qts):
                    kcols = TK if qi >= 9 else 128 * (qi + 1)
                    stok = p.op("vector", [],
                                lambda e, i=i, kcols=kcols: e.tensor_scalar(
                                    attnq[:, TK * i:TK * i + kcols],
                                    attnq[:, TK * i:TK * i + kcols],
                                    rsum[:, 4 + i:5 + i], None,
                                    op0=AluOpType.mult, op1=AluOpType.bypass),
                                bump=dv)
                    scale_tok[i] = stok
                    prev_scale[i] = stok
                if ph2l < 3:
                    continue
                mmax = min(NKT - 1, 4 * qb + 3)
                tr_ev = []
                ac_bar = ac.n
                for m in range(mmax + 1):
                    bT = 4 + m % 2
                    for i in range(4):
                        w = [(dv, scale_tok[i])]
                        if i == 0:
                            if m == 0:
                                w.append((ac, ac_bar))
                            if last_trT[m % 2] is not None:
                                w.append(last_trT[m % 2])
                        tok = p.op("tensor", w,
                                   lambda e, m=m, i=i, bT=bT: e.transpose(
                                       ps[bT][:, 128 * i:128 * (i + 1)],
                                       attnq[:, TK * i + 128 * m:
                                             TK * i + 128 * (m + 1)],
                                       identf[:]),
                                   bump=(pe if i == 3 else None))
                    etok = p.op("scalar", [(pe, tok)],
                                lambda e, m=m, bT=bT: e.activation(
                                    out=attnT[:, 512 * m:512 * (m + 1)],
                                    in_=ps[bT][:, 0:512],
                                    func=AF.Copy, scale=1.0), bump=ac)
                    last_trT[m % 2] = (ac, etok)
                    last_exp[bT] = (ac, etok)
                    tr_ev.append(etok)
                prev_transposes = tok
                if ph2l < 4:
                    continue
                for m in range(mmax + 1):
                    vt = 0 if m == 0 else m + 7
                    lhs = vtm[:, 512 * VSLOT[vt] + 128 * kv:
                              512 * VSLOT[vt] + 128 * (kv + 1)]
                    w = [(ac, tr_ev[m])]
                    if m == 0 and avfree is not None:
                        w.append(avfree)
                    tok = p.op("tensor", w,
                               lambda e, lhs=lhs, m=m, mmax=mmax: e.matmul(
                                   ps[6][:, 0:512], lhs,
                                   attnT[:, 512 * m:512 * (m + 1)],
                                   start=(m == 0), stop=(m == mmax)),
                               bump=(pe if m == mmax else None))
                dst = qav[:, 2048 * h + 512 * qb:2048 * h + 512 * (qb + 1)]
                tok = p.op("scalar", [(pe, tok)],
                           lambda e, dst=dst: e.activation(
                               out=dst, in_=ps[6][:, 0:512], func=AF.Copy,
                               scale=1.0), bump=ac)
                avfree = (ac, tok)
                emit_mat_piece()
        while mat_state["i"] < len(mat_pieces):
            emit_mat_piece()
        mat_stores = gmt.n

        def cc_call(e, ins, outs):
            return e.collective_compute("AllGather", AluOpType.bypass,
                                        ins=ins, outs=outs,
                                        replica_groups=[list(range(8))])
        p.op("gpsimd", [(gmt, mat_stores)],
             lambda e: cc_call(e, [ccg_in[:]], [ccg[:]]), bump=cc)
        p.op("gpsimd", [], lambda e: cc_call(e, [ccu_in[:]], [ccu[:]]), bump=cc)
        p.op("gpsimd", [], lambda e: cc_call(e, [ccd_in[:]], [ccd[:]]), bump=cc)
        ag_done = cc.n
        attn_ac = ac.n
        ph2.close()

        if taps:
            p.dma("gpsimd", gq, av_o[:], qav[:],
                  waits=[(pe, pe.n), (dv, dv.n), (ac, ac.n)])
            p.dma("gpsimd", gq, aq_o[:], attnq[:])
            p.dma("gpsimd", gq, aT_o[:], attnT[:])
            p.dma("gpsimd", gq, sums_o[:], sums[:])
            p.dma("gpsimd", gq, rsum_o[:], rsum[:])
            sct = p.op("scalar", [(pe, pe.n), (ac, ac.n)],
                       lambda e: e.activation(out=scdbg[:], in_=ps[3][:],
                                              func=AF.Copy, scale=1.0), bump=ac)
            p.dma("gpsimd", gq, sc_o[:], scdbg[:], waits=[(ac, sct)])

        if phases <= 2:
            ph2hold.close()
            p.dma("gpsimd", gq, out_h[0:128, 0:128], maskt[:])
            _finish()
            return nc

        ph3 = contextlib.ExitStack()
        wob = ph3.enter_context(nc.sbuf_tensor("wob", [128, 2 * 2048], BF16))
        xo = ph3.enter_context(nc.sbuf_tensor("xo", [128, 2 * 512], F32))
        x2st = ph3.enter_context(nc.sbuf_tensor("x2st", [128, 2 * 512], BF16))
        x2a = ph3.enter_context(nc.sbuf_tensor("x2a", [128, 2 * 512], BF16))
        x2sq = ph3.enter_context(nc.sbuf_tensor("x2sq", [128, 2 * 512], BF16))
        st4 = ph3.enter_context(nc.sbuf_tensor("st4", [1, 1024], F32))

        ofree = {0: None, 1: None}
        x2_store_tok = {0: None, 1: None}
        wo_free = {0: None, 1: None}
        prev_smm = None
        for tc in range(4):
            dt_mm_tok = {}
            tc_store_tok = []
            for dt in range(ND):
                bk = dt % 2
                w = [(ac, attn_ac)] if (tc == 0 and dt < 2) else []
                if wo_free[bk] is not None:
                    w.append(wo_free[bk])
                p.dma("sync", qw[bk], wob[:, 2048 * bk:2048 * bk + 1024],
                      mkap(wo_bf, dt * 262144,
                           [(128, 128), (16384, 8), (1, 128)]), waits=w)
                p.dma("sync", qw[bk], wob[:, 2048 * bk + 1024:2048 * bk + 2048],
                      mkap(wo_bf, dt * 262144 + 8 * 16384,
                           [(128, 128), (16384, 8), (1, 128)]))
                wo_ld = (qw[bk], qw[bk].n)
                pso = ps[4 + bk][:]
                xbuf = xo[:, 512 * bk:512 * (bk + 1)]
                x2b = x2st[:, 512 * bk:512 * (bk + 1)]
                w_ld = [ofree[bk]] if ofree[bk] else [(ac, attn_ac)]
                p.dma("gpsimd", gx[bk], xbuf,
                      xT[128 * dt:128 * (dt + 1), 512 * tc:512 * (tc + 1)],
                      waits=w_ld)
                xld = (gx[bk], gx[bk].n)
                w = [wo_ld]
                if ofree[bk] is not None:
                    w.append(ofree[bk])
                else:
                    w += [free[4 + bk], (ac, attn_ac)]
                for hh in range(H):
                    tok = p.op("tensor", w if hh == 0 else [],
                               lambda e, pso=pso, hh=hh, bk=bk, tc=tc:
                               e.matmul(
                                   pso,
                                   wob[:, 2048 * bk + 128 * hh:
                                       2048 * bk + 128 * (hh + 1)],
                                   qav[:, 2048 * hh + 512 * tc:
                                       2048 * hh + 512 * (tc + 1)],
                                   start=(hh == 0), stop=(hh == H - 1)),
                               bump=(pe if hh == H - 1 else None))
                dt_mm_tok[dt] = tok
                wo_free[bk] = (pe, tok)
                wv_ = [(pe, tok), xld]
                if x2_store_tok[bk] is not None:
                    wv_.append(x2_store_tok[bk])
                dtok = p.op("vector", wv_,
                            lambda e, x2b=x2b, xbuf=xbuf, pso=pso:
                            e.tensor_tensor(x2b, xbuf, pso, op=AluOpType.add),
                            bump=dv)
                ofree[bk] = (dv, dtok)
                p.dma("gpsimd", gst[bk],
                      x2h[128 * dt:128 * (dt + 1), 512 * tc:512 * (tc + 1)],
                      x2b, waits=[(dv, dtok)])
                x2_store_tok[bk] = (gst[bk], gst[bk].n)
                tc_store_tok.append(x2_store_tok[bk])
            smm_tok = []
            for dt in range(ND):
                buf = dt % 2
                w = [tc_store_tok[dt]]
                if smm_tok[:max(0, dt - 1)]:
                    pass
                if dt >= 2:
                    w.append((pe, smm_tok[dt - 2]))
                elif prev_smm is not None:
                    w.append((pe, prev_smm))
                p.dma("gpsimd", gx[buf], x2a[:, 512 * buf:512 * (buf + 1)],
                      x2h[128 * dt:128 * (dt + 1), 512 * tc:512 * (tc + 1)],
                      waits=w)
                ld = (gx[buf], gx[buf].n)
                xb = x2a[:, 512 * buf:512 * (buf + 1)]
                sb = x2sq[:, 512 * buf:512 * (buf + 1)]
                tok = p.op("vector", [ld],
                           lambda e, sb=sb, xb=xb: e.tensor_tensor(
                               sb, xb, xb, op=AluOpType.mult), bump=dv)
                w2 = [(dv, tok)]
                if dt == 0:
                    w2 += [free[0], free[1], (pl, tok_ones)]
                p.op("tensor", w2,
                     lambda e, xb=xb, dt=dt: e.matmul(
                         ps[0][0:1, :], onesb[:], xb,
                         start=(dt == 0), stop=(dt == ND - 1)))
                tok = p.op("tensor", [],
                           lambda e, sb=sb, dt=dt: e.matmul(
                               ps[1][0:1, :], onesb[:], sb,
                               start=(dt == 0), stop=(dt == ND - 1)), bump=pe)
                smm_tok.append(tok)
            prev_smm = smm_tok[-1]
            stats_pe = pe.n
            mu_s = mu2r[0:1, 512 * tc:512 * (tc + 1)]
            r_s = mu2r[0:1, T + 512 * tc:T + 512 * (tc + 1)]
            m2_s = st4[0:1, 0:512]
            sg_s = st4[0:1, 512:1024]
            p.op("vector", [(pe, stats_pe)],
                 lambda e, mu_s=mu_s: e.tensor_scalar(
                     mu_s, ps[0][0:1, :], 1.0 / D, None,
                     op0=AluOpType.mult, op1=AluOpType.bypass))
            p.op("vector", [],
                 lambda e, m2_s=m2_s: e.tensor_scalar(
                     m2_s, ps[1][0:1, :], 1.0 / D, None,
                     op0=AluOpType.mult, op1=AluOpType.bypass))
            p.op("vector", [],
                 lambda e, sg_s=sg_s, mu_s=mu_s: e.tensor_tensor(
                     sg_s, mu_s, mu_s, op=AluOpType.mult))
            tok = p.op("vector", [],
                       lambda e, m2_s=m2_s, sg_s=sg_s: e.scalar_tensor_tensor(
                           m2_s, m2_s, EPS, sg_s,
                           op0=AluOpType.add, op1=AluOpType.subtract), bump=dv)
            free[0] = free[1] = (dv, tok)
            tok = p.op("scalar", [(dv, tok)],
                       lambda e, sg_s=sg_s, m2_s=m2_s: e.activation(
                           out=sg_s, in_=m2_s, func=AF.Sqrt, scale=1.0),
                       bump=ac)
            p.op("vector", [(ac, tok)],
                 lambda e, r_s=r_s, sg_s=sg_s: e.reciprocal(r_s, sg_s), bump=dv)
        x2_done = [(gst[0], gst[0].n), (gst[1], gst[1].n)]
        free[4] = ofree[0]
        free[5] = ofree[1]
        oproj_pe = pe.n
        oproj_dv = dv.n
        ln2_rows_dv = dv.n
        ph3.close()
        ph2hold.close()

        ph5 = contextlib.ExitStack()
        hbuf = ph5.enter_context(nc.sbuf_tensor("hbuf", [128, NI * 1024], BF16))
        ln2b = ph5.enter_context(nc.sbuf_tensor("ln2b", [128, ND * 1024], BF16))
        wgb = ph5.enter_context(nc.sbuf_tensor("wgb", [128, 2 * 2048], BF16))
        wub = ph5.enter_context(nc.sbuf_tensor("wub", [128, 2 * 2048], BF16))
        wdb = ph5.enter_context(nc.sbuf_tensor("wdb", [128, 2 * NI * 128], BF16))
        x2f = ph5.enter_context(nc.sbuf_tensor("x2f", [128, 2 * 512], BF16))
        sgs = ph5.enter_context(nc.sbuf_tensor("sgs", [128, 2 * 512], BF16))
        lnf = ph5.enter_context(nc.sbuf_tensor("lnf", [128, 2 * 512], F32))
        outs = ph5.enter_context(nc.sbuf_tensor("outs", [128, 2 * 512], F32))

        first_ffn_ld = True
        dbg_half = halves
        out_store_tok = {0: None, 1: None}
        sil_tok = {0: None, 1: None}
        hmul_tok = {0: None, 1: None}
        wg_free = {0: None, 1: None}
        wd_free = {0: None, 1: None}
        dn_free = {0: None, 1: None}
        for half in range(dbg_half):
            x2f_guard = (dv, dv.n)
            apply_dv_tok = []
            for tcl in range(2):
                tc = 2 * half + tcl
                mu_s = mu2r[0:1, 512 * tc:512 * (tc + 1)]
                r_s = mu2r[0:1, T + 512 * tc:T + 512 * (tc + 1)]
                bcast_row(mu_s, 2, [(dv, ln2_rows_dv)],
                          [(pl, tok_ones), free[2], free[3]])
                bc_tok = bcast_row(r_s, 3, [], [])
                for dt in range(ND):
                    buf = dt % 2
                    if len(apply_dv_tok) >= 2:
                        w = [(dv, apply_dv_tok[-2])]
                    else:
                        w = [x2f_guard]
                    p.dma("gpsimd", gx[buf], x2f[:, 512 * buf:512 * (buf + 1)],
                          x2h[128 * dt:128 * (dt + 1), 512 * tc:512 * (tc + 1)],
                          waits=w)
                    xb = x2f[:, 512 * buf:512 * (buf + 1)]
                    w2 = [(gx[buf], gx[buf].n)]
                    if dt == 0:
                        w2.append((pe, bc_tok))
                    dst = ln2b[:, 1024 * dt + 512 * tcl:1024 * dt + 512 * (tcl + 1)]
                    lf = lnf[:, 512 * buf:512 * (buf + 1)]
                    p.op("vector", w2,
                         lambda e, lf=lf, xb=xb: e.tensor_tensor(
                             lf, xb, ps[2][:], op=AluOpType.subtract))
                    tok = p.op("vector", [],
                               lambda e, dst=dst, lf=lf: e.tensor_tensor(
                                   dst, lf, ps[3][:], op=AluOpType.mult),
                               bump=dv)
                    apply_dv_tok.append(tok)
                free[2] = free[3] = (dv, apply_dv_tok[-1])
            ln2_apply_dv = dv.n

            for it in range(NI):
                par = it % 2
                w = []
                if first_ffn_ld:
                    w += [(cc, ag_done), (pe, oproj_pe)]
                    first_ffn_ld = False
                if wg_free[par] is not None:
                    w.append(wg_free[par])
                cst = NI * 16384
                p.dma("sync", qg[par], wgb[:, 2048 * par:2048 * par + 1024],
                      mkap(ccg, it * 16384,
                           [(128, 128), (cst, 8), (1, 128)]), waits=w)
                p.dma("sync", qg[par], wgb[:, 2048 * par + 1024:2048 * (par + 1)],
                      mkap(ccg, it * 16384 + 8 * cst,
                           [(128, 128), (cst, 8), (1, 128)]))
                p.dma("sync", qu[par], wub[:, 2048 * par:2048 * par + 1024],
                      mkap(ccu, it * 16384,
                           [(128, 128), (cst, 8), (1, 128)]))
                p.dma("sync", qu[par], wub[:, 2048 * par + 1024:2048 * (par + 1)],
                      mkap(ccu, it * 16384 + 8 * cst,
                           [(128, 128), (cst, 8), (1, 128)]))
                wld = [(qg[par], qg[par].n), (qu[par], qu[par].n)]
                gu_tok = None
                for tcl in range(2):
                    gps = ps[par][:]
                    ups = ps[4 + par][:]
                    w = list(wld)
                    if it < 2 and tcl == 0:
                        w.append((dv, ln2_apply_dv))
                        w += [free[par], free[4 + par]]
                    if sil_tok[par] is not None:
                        w.append(sil_tok[par])
                    if hmul_tok[par] is not None:
                        w.append(hmul_tok[par])
                    for c in range(ND):
                        p.op("tensor", w if c == 0 else [],
                             lambda e, gps=gps, par=par, c=c, tcl=tcl: e.matmul(
                                 gps,
                                 wgb[:, 2048 * par + 128 * c:
                                     2048 * par + 128 * (c + 1)],
                                 ln2b[:, 1024 * c + 512 * tcl:
                                      1024 * c + 512 * (tcl + 1)],
                                 start=(c == 0), stop=(c == ND - 1)))
                        tok = p.op("tensor", [],
                                   lambda e, ups=ups, par=par, c=c, tcl=tcl:
                                   e.matmul(
                                       ups,
                                       wub[:, 2048 * par + 128 * c:
                                           2048 * par + 128 * (c + 1)],
                                       ln2b[:, 1024 * c + 512 * tcl:
                                            1024 * c + 512 * (tcl + 1)],
                                       start=(c == 0), stop=(c == ND - 1)),
                                   bump=(pe if c == ND - 1 else None))
                    gu_tok = tok
                    sg_b = sgs[:, 512 * par:512 * (par + 1)]
                    w3 = [(pe, gu_tok)]
                    if hmul_tok[par] is not None:
                        w3.append(hmul_tok[par])
                    stok = p.op("scalar", w3,
                                lambda e, sg_b=sg_b, gps=gps: e.activation(
                                    out=sg_b, in_=gps, func=AF.Silu, scale=1.0),
                                bump=ac)
                    sil_tok[par] = (ac, stok)
                    dst = hbuf[:, 1024 * it + 512 * tcl:1024 * it + 512 * (tcl + 1)]
                    htok = p.op("vector", [(ac, stok)],
                                lambda e, dst=dst, sg_b=sg_b, ups=ups:
                                e.tensor_tensor(dst, sg_b, ups,
                                                op=AluOpType.mult), bump=dv)
                    hmul_tok[par] = (dv, htok)
                wg_free[par] = (pe, gu_tok)
            gup_dv = dv.n
            free[0] = hmul_tok[0] or free[0]
            free[1] = hmul_tok[1] or free[1]
            free[4] = sil_tok[0]
            free[5] = sil_tok[1]

            for dt in range(ND):
                par = dt % 2
                w = [wd_free[par]] if wd_free[par] else []
                for spl in range(6):
                    b0 = 8 * spl
                    b1 = min(NI, b0 + 8)
                    p.dma("sync", qd[par],
                          wdb[:, NI * 128 * par + 128 * b0:
                              NI * 128 * par + 128 * b1],
                          mkap(ccd, b0 * 262144 + dt * 16384,
                               [(128, 128), (262144, b1 - b0), (1, 128)]),
                          waits=(w if spl == 0 else []))
                wld = (qd[par], qd[par].n)
                for tcl in range(2):
                    tc = 2 * half + tcl
                    dps = ps[2 + par][:]
                    w = [wld]
                    if dt < 2 and tcl == 0:
                        w += [(dv, gup_dv), free[2 + par]]
                    if dn_free[par] is not None:
                        w.append(dn_free[par])
                    for j in range(NI):
                        tok = p.op("tensor", w if j == 0 else [],
                                   lambda e, dps=dps, par=par, j=j, tcl=tcl:
                                   e.matmul(
                                       dps,
                                       wdb[:, NI * 128 * par + 128 * j:
                                           NI * 128 * par + 128 * (j + 1)],
                                       hbuf[:, 1024 * j + 512 * tcl:
                                            1024 * j + 512 * (tcl + 1)],
                                       start=(j == 0), stop=(j == NI - 1)),
                                   bump=(pe if j == NI - 1 else None))
                    dn_tok = tok
                    wx = [dn_free[par]] if dn_free[par] is not None else []
                    if dt < 2 and tcl == 0:
                        wx.append((dv, gup_dv))
                    p.dma("gpsimd", gx[par], x2f[:, 512 * par:512 * (par + 1)],
                          x2h[128 * dt:128 * (dt + 1), 512 * tc:512 * (tc + 1)],
                          waits=wx)
                    ld = (gx[par], gx[par].n)
                    ob = outs[:, 512 * par:512 * (par + 1)]
                    wv_ = [(pe, dn_tok), ld]
                    if out_store_tok[par] is not None:
                        wv_.append(out_store_tok[par])
                    otok = p.op("vector", wv_,
                                lambda e, ob=ob, par=par, dps=dps:
                                e.tensor_tensor(
                                    ob, x2f[:, 512 * par:512 * (par + 1)], dps,
                                    op=AluOpType.add), bump=dv)
                    dn_free[par] = (dv, otok)
                    p.dma("gpsimd", gst[par],
                          out_h[128 * dt:128 * (dt + 1), 512 * tc:512 * (tc + 1)],
                          ob, waits=[(dv, otok)])
                    out_store_tok[par] = (gst[par], gst[par].n)
                wd_free[par] = (pe, dn_tok)
            free[2] = dn_free[0]
            free[3] = dn_free[1]
        if taps:
            p.dma("gpsimd", gq, hb_o[:], hbuf[:],
                  waits=[(pe, pe.n), (dv, dv.n), (ac, ac.n)])
            p.dma("gpsimd", gq, ln2b_o[:], ln2b[:])
        ph5.close()
        _finish()
    return nc


_NC_CACHE = None


def _get_nc():
    global _NC_CACHE
    if _NC_CACHE is None:
        _NC_CACHE = build_nc()
    return _NC_CACHE


def _host_constants():
    inv = 1.0 / (10000.0 ** (np.arange(0, DH, 2, dtype=np.float64) / DH))
    pos = np.arange(T, dtype=np.float64)
    ang = pos[None, :] * inv[:, None]
    cosp = np.cos(ang)
    sinp = np.sin(ang)
    cosE = np.empty((128, T), dtype=np.float32)
    sinE = np.empty((128, T), dtype=np.float32)
    cosE[0::2] = cosp
    cosE[1::2] = cosp
    sinE[0::2] = -sinp
    sinE[1::2] = sinp
    ident = np.eye(128, dtype=ml_dtypes.bfloat16)
    perm = np.zeros((128, 128), dtype=ml_dtypes.bfloat16)
    for i in range(128):
        perm[i, i ^ 1] = 1.0
    mask = np.where(np.arange(128)[None, :] <= np.arange(128)[:, None],
                    np.float32(0.0), np.float32(NEG))
    return (cosE.astype(ml_dtypes.bfloat16), sinE.astype(ml_dtypes.bfloat16),
            ident, perm, np.ascontiguousarray(mask.astype(np.float32)))


def kernel(L0_attn_x, L0_attn_Wq, L0_attn_Wk, L0_attn_Wv, L0_attn_Wo,
           L0_ffn_lut, L0_ffn_gate_idx, L0_ffn_up_idx, L0_ffn_down_idx):
    x = np.asarray(L0_attn_x, dtype=np.float32)
    wq = np.ascontiguousarray(np.asarray(L0_attn_Wq, dtype=np.float32))
    wk = np.ascontiguousarray(np.asarray(L0_attn_Wk, dtype=np.float32))
    wv = np.ascontiguousarray(np.asarray(L0_attn_Wv, dtype=np.float32))
    wo = np.ascontiguousarray(np.asarray(L0_attn_Wo, dtype=np.float32))
    lut = np.asarray(L0_ffn_lut, dtype=np.float32).reshape(16)
    gidxT = np.ascontiguousarray(np.asarray(L0_ffn_gate_idx, dtype=np.int32).T)
    uidxT = np.ascontiguousarray(np.asarray(L0_ffn_up_idx, dtype=np.int32).T)
    didxT = np.ascontiguousarray(np.asarray(L0_ffn_down_idx, dtype=np.int32).T)
    didxT_pad = np.zeros((8 * DSLOTS * 128, D), dtype=np.int32)
    didxT_pad[:INTER] = didxT

    lutrep = np.ascontiguousarray(np.broadcast_to(lut[None, :], (128, 16)))
    cosE, sinE, ident, perm, mask = _host_constants()

    nc = _get_nc()
    in_maps = []
    for c in range(8):
        in_maps.append({
            "xT": np.ascontiguousarray(x[c].T),
            "wq": wq, "wk": wk, "wv": wv, "wo": wo,
            "gsh": np.ascontiguousarray(gidxT[256 * c:256 * (c + 1)]),
            "ush": np.ascontiguousarray(uidxT[256 * c:256 * (c + 1)]),
            "dsh": np.ascontiguousarray(
                didxT_pad[DSLOTS * 128 * c:DSLOTS * 128 * (c + 1)]),
            "lutrep": lutrep,
            "cosE": cosE, "sinE": sinE,
            "identb": ident, "identf": np.eye(128, dtype=np.float32),
            "permP": perm, "maskt": mask,
        })
    res = run_bass_kernel_spmd(nc, in_maps, core_ids=list(range(8)))
    out = np.empty((B, T, D), dtype=np.float32)
    for c in range(8):
        out[c] = res.results[c]["out"].T
    return out


if __name__ == "__main__":
    build_nc()
    print("build ok")


# version 10
# speedup vs baseline: 1.0163x; 1.0163x over previous
import math
import contextlib
import numpy as np
import ml_dtypes

import concourse.bass as bass
import concourse.mybir as mybir
from concourse.bass_utils import run_bass_kernel_spmd
from concourse.alu_op_type import AluOpType

F32 = mybir.dt.float32
BF16 = mybir.dt.bfloat16
I32 = mybir.dt.int32
AF = mybir.ActivationFunctionType

B = 8
T = 2048
D = 2048
H = 16
KVH = 4
DH = 128
INTER = 5504
TK = 1152
NT = T // 128
ND = D // 128
NI = INTER // 128
NKT = TK // 128
EPS = 1e-5
QSCALE = 1.0 / math.sqrt(DH)
NEG = -1.0e9
DSLOTS = 6

VTILES = [0] + list(range(8, 16))
VSLOT = {t_: i for i, t_ in enumerate(VTILES)}


class Sem:
    def __init__(self, stack, nc, name):
        self.h = stack.enter_context(nc.semaphore(name))
        self.n = 0


class Prog:

    ENGINES = ("sync", "gpsimd", "vector", "scalar", "tensor")

    def __init__(self):
        self.th = {e: [] for e in self.ENGINES}

    def op(self, eng, waits, build, bump=None, k=1):
        ws = [(s.h, v) for (s, v) in
              (x for x in waits if x is not None)
              if s is not None and v is not None]
        tok = None
        hb = None
        if bump is not None:
            bump.n += k
            tok = bump.n
            hb = bump.h

        def th(e, ws=ws, build=build, hb=hb, k=k):
            for (sh, v) in ws:
                e.wait_ge(sh, v)
            inst = build(e)
            if hb is not None:
                inst.then_inc(hb, k)
        self.th[eng].append(th)
        return tok

    def dma(self, eng, q, out, in_, waits=()):
        return self.op(eng, waits,
                       lambda e, out=out, in_=in_: e.dma_start(out=out, in_=in_),
                       bump=q, k=16)

    def wait(self, eng, *waits):
        ws = [(s.h, v) for (s, v) in
              (x for x in waits if x is not None)
              if s is not None and v is not None]

        def th(e, ws=ws):
            for (sh, v) in ws:
                e.wait_ge(sh, v)
        self.th[eng].append(th)

    def finalize(self, blk):
        for name in self.ENGINES:
            thunks = self.th[name]
            if not thunks:
                continue
            dec = getattr(blk, name)

            @dec
            def _(e, thunks=thunks):
                for t in thunks:
                    t(e)


def mkap(base_ap, off, dims):
    return bass.AP(base_ap.tensor, base_ap.offset + off, [list(d) for d in dims])


PH2L = 4


def build_nc(phases=9, ph2l=None, taps=False, tbs=4, halves=2):
    if ph2l is None:
        ph2l = PH2L
    nc = bass.Bass("TRN2", target_bir_lowering=False)

    xT = nc.dram_tensor("xT", [D, T], F32, kind="ExternalInput").ap()
    wq = nc.dram_tensor("wq", [D, D], F32, kind="ExternalInput").ap()
    wk = nc.dram_tensor("wk", [D, KVH * DH], F32, kind="ExternalInput").ap()
    wv = nc.dram_tensor("wv", [D, KVH * DH], F32, kind="ExternalInput").ap()
    wo = nc.dram_tensor("wo", [D, D], F32, kind="ExternalInput").ap()
    gsh = nc.dram_tensor("gsh", [256, INTER], I32, kind="ExternalInput").ap()
    ush = nc.dram_tensor("ush", [256, INTER], I32, kind="ExternalInput").ap()
    dsh = nc.dram_tensor("dsh", [DSLOTS * 128, D], I32, kind="ExternalInput").ap()
    lutrep_h = nc.dram_tensor("lutrep", [128, 16], F32, kind="ExternalInput").ap()
    cosE_h = nc.dram_tensor("cosE", [128, T], BF16, kind="ExternalInput").ap()
    sinE_h = nc.dram_tensor("sinE", [128, T], BF16, kind="ExternalInput").ap()
    identb_h = nc.dram_tensor("identb", [128, 128], BF16, kind="ExternalInput").ap()
    identf_h = nc.dram_tensor("identf", [128, 128], F32, kind="ExternalInput").ap()
    permP_h = nc.dram_tensor("permP", [128, 128], BF16, kind="ExternalInput").ap()
    maskt_h = nc.dram_tensor("maskt", [128, 128], F32, kind="ExternalInput").ap()
    out_h = nc.dram_tensor("out", [D, T], F32, kind="ExternalOutput").ap()
    if taps:
        qav_o = nc.dram_tensor("qav_o", [128, H * T], BF16, kind="ExternalOutput").ap()
        kTr_o = nc.dram_tensor("kTr_o", [128, KVH * T], BF16, kind="ExternalOutput").ap()
        vtm_o = nc.dram_tensor("vtm_o", [128, 9 * 512], BF16, kind="ExternalOutput").ap()
        av_o = nc.dram_tensor("av_o", [128, H * T], BF16, kind="ExternalOutput").ap()
        x2_o = nc.dram_tensor("x2_o", [D, T], BF16, kind="ExternalOutput").ap()
        ln2_o = nc.dram_tensor("ln2_o", [1, 2 * T], F32, kind="ExternalOutput").ap()
        ln1_o = nc.dram_tensor("ln1_o", [128, ND * 512], BF16, kind="ExternalOutput").ap()
        aq_o = nc.dram_tensor("aq_o", [128, 4 * TK], F32, kind="ExternalOutput").ap()
        aT_o = nc.dram_tensor("aT_o", [128, NKT * 512], BF16, kind="ExternalOutput").ap()
        sums_o = nc.dram_tensor("sums_o", [128, 16], F32, kind="ExternalOutput").ap()
        rsum_o = nc.dram_tensor("rsum_o", [128, 8], F32, kind="ExternalOutput").ap()
        sc_o = nc.dram_tensor("sc_o", [128, 512], F32, kind="ExternalOutput").ap()
        ccg_o = nc.dram_tensor("ccg_o", [ND, NI, 128, 128], BF16, kind="ExternalOutput").ap()
        ccd_o = nc.dram_tensor("ccd_o", [8 * DSLOTS, ND, 128, 128], BF16, kind="ExternalOutput").ap()
        hb_o = nc.dram_tensor("hb_o", [128, NI * 1024], BF16, kind="ExternalOutput").ap()
        ln2b_o = nc.dram_tensor("ln2b_o", [128, ND * 1024], BF16, kind="ExternalOutput").ap()

    wq_bf = nc.dram_tensor("wq_bf", [ND, D, 128], BF16, kind="Internal").ap()
    wo_bf = nc.dram_tensor("wo_bf", [ND, D, 128], BF16, kind="Internal").ap()
    wk_img = nc.dram_tensor("wk_img", [128, KVH * ND * 128], BF16, kind="Internal").ap()
    wv_img = nc.dram_tensor("wv_img", [128, ND * KVH * DH], BF16, kind="Internal").ap()
    x2h = nc.dram_tensor("x2h", [D, T], BF16, kind="Internal").ap()
    ccg_in = nc.dram_tensor("ccg_in", [2, NI, 128, 128], BF16, kind="Internal").ap()
    ccu_in = nc.dram_tensor("ccu_in", [2, NI, 128, 128], BF16, kind="Internal").ap()
    ccd_in = nc.dram_tensor("ccd_in", [DSLOTS, ND, 128, 128], BF16, kind="Internal").ap()
    ccg = nc.dram_tensor("ccg", [ND, NI, 128, 128], BF16, kind="Internal",
                         addr_space="Shared").ap()
    ccu = nc.dram_tensor("ccu", [ND, NI, 128, 128], BF16, kind="Internal",
                         addr_space="Shared").ap()
    ccd = nc.dram_tensor("ccd", [8 * DSLOTS, ND, 128, 128], BF16, kind="Internal",
                         addr_space="Shared").ap()

    p = Prog()
    top = contextlib.ExitStack()
    with top:
        def _finish():
            p.wait("gpsimd", (gq, gq.n), (gst[0], gst[0].n), (gst[1], gst[1].n),
                   (gcv, gcv.n), (gmt, gmt.n))
            _b = top.enter_context(nc.Block())
            p.finalize(_b)
        qcst = Sem(top, nc, "qcst")
        qc = [Sem(top, nc, "qc0"), Sem(top, nc, "qc1")]
        qis = [Sem(top, nc, "qi0"), Sem(top, nc, "qi1")]
        qx = [Sem(top, nc, "qx0"), Sem(top, nc, "qx1")]
        qa = [Sem(top, nc, "qa0"), Sem(top, nc, "qa1")]
        qw = [Sem(top, nc, "qw0"), Sem(top, nc, "qw1")]
        qkv = Sem(top, nc, "qkv")
        qg = [Sem(top, nc, "qg0"), Sem(top, nc, "qg1")]
        qu = [Sem(top, nc, "qu0"), Sem(top, nc, "qu1")]
        qd = [Sem(top, nc, "qd0"), Sem(top, nc, "qd1")]
        gx = [Sem(top, nc, "gx0"), Sem(top, nc, "gx1")]
        gst = [Sem(top, nc, "gst0"), Sem(top, nc, "gst1")]
        gcv = Sem(top, nc, "gcv")
        gmt = Sem(top, nc, "gmt")
        gq = Sem(top, nc, "gq")
        pe = Sem(top, nc, "pe")
        dv = Sem(top, nc, "dv")
        ac = Sem(top, nc, "ac")
        pl = Sem(top, nc, "pl")
        cc = Sem(top, nc, "cc")

        identb = top.enter_context(nc.sbuf_tensor("identb_s", [128, 128], BF16))
        identf = top.enter_context(nc.sbuf_tensor("identf_s", [128, 128], F32))
        permP = top.enter_context(nc.sbuf_tensor("permP_s", [128, 128], BF16))
        maskt = top.enter_context(nc.sbuf_tensor("maskt_s", [128, 128], F32))
        luts = top.enter_context(nc.sbuf_tensor("luts", [128, 16], F32))
        lutsD = top.enter_context(nc.sbuf_tensor("lutsD", [128, 18], F32))
        onesb = top.enter_context(nc.sbuf_tensor("onesb", [128, 1], BF16))
        onesf = top.enter_context(nc.sbuf_tensor("onesf", [1, 128], F32))
        mu2r = top.enter_context(nc.sbuf_tensor("mu2r", [1, 2 * T], F32))
        scdbg = top.enter_context(nc.sbuf_tensor("scdbg", [128, 512], F32))
        ones1b = top.enter_context(nc.sbuf_tensor("ones1b", [1, 128], BF16))
        bsp = top.enter_context(nc.sbuf_tensor("bsp", [1, 3 * 512], BF16))
        tsp = top.enter_context(nc.sbuf_tensor("tsp", [1, 2 * 512], F32))

        ps = [top.enter_context(nc.psum_tensor(f"ps{i}", [128, 512], F32))
              for i in range(7)]
        free = {i: None for i in range(7)}

        phR = contextlib.ExitStack()
        cosE = phR.enter_context(nc.sbuf_tensor("cosE_s", [128, T], BF16))
        sinE = phR.enter_context(nc.sbuf_tensor("sinE_s", [128, T], BF16))

        ph0 = contextlib.ExitStack()
        wstage = ph0.enter_context(nc.sbuf_tensor("wstage", [128, 2 * D], F32))
        wstageb = ph0.enter_context(nc.sbuf_tensor("wstageb", [128, 2 * D], BF16))
        idxs = ph0.enter_context(nc.sbuf_tensor("idxs", [128, 2 * INTER], I32))
        idxbf = ph0.enter_context(nc.sbuf_tensor("idxbf", [128, INTER], BF16))
        macc = ph0.enter_context(nc.sbuf_tensor("macc", [128, 2 * INTER], BF16))
        mmp = ph0.enter_context(nc.sbuf_tensor("mmp", [128, INTER], BF16))

        for dst, src in [(cosE[:], cosE_h[:]), (sinE[:], sinE_h[:]),
                         (identb[:], identb_h[:]), (identf[:], identf_h[:]),
                         (permP[:], permP_h[:]), (maskt[:], maskt_h[:]),
                         (luts[:], lutrep_h[:])]:
            p.dma("sync", qcst, dst, src)
        consts_full = (qcst, qcst.n)
        p.op("vector", [consts_full],
             lambda e: e.tensor_reduce(lutsD[:, 16:17], luts[:],
                                       axis=mybir.AxisListType.X,
                                       op=AluOpType.add))
        p.op("vector", [], lambda e: e.memset(lutsD[:, 17:18], 0.0))
        p.op("vector", [],
             lambda e: e.tensor_scalar(lutsD[:, 16:17], lutsD[:, 16:17],
                                       1.0 / 16.0, None, op0=AluOpType.mult,
                                       op1=AluOpType.bypass))
        p.op("vector", [], lambda e: e.memset(lutsD[:, 17:18], 0.0))
        p.op("vector", [], lambda e: e.memset(lutsD[:, 17:18], 0.0))
        lut_prep = p.op("vector", [],
                        lambda e: e.tensor_scalar(
                            lutsD[:, 0:16], luts[:], lutsD[:, 16:17], None,
                            op0=AluOpType.subtract, op1=AluOpType.bypass),
                        bump=dv)
        p.op("gpsimd", [], lambda e: e.memset(onesf[:], 1.0), bump=pl)
        p.op("gpsimd", [], lambda e: e.memset(ones1b[:], 1.0), bump=pl)
        tok_ones = p.op("gpsimd", [], lambda e: e.memset(onesb[:], 1.0), bump=pl)

        conv_jobs = []
        for c in range(ND):
            conv_jobs.append((wq[128 * c:128 * (c + 1), :], D,
                              [((wq_bf, c), "blk")]))
        for c in range(ND):
            conv_jobs.append((wo[128 * c:128 * (c + 1), :], D,
                              [((wo_bf, c), "blk")]))
        for c in range(ND):
            dsts = [(wk_img[:, dk * 2048 + c * 128: dk * 2048 + c * 128 + 128],
                     ("col", dk)) for dk in range(KVH)]
            conv_jobs.append((wk[128 * c:128 * (c + 1), :], KVH * DH, dsts))
        for c in range(ND):
            conv_jobs.append((wv[128 * c:128 * (c + 1), :], KVH * DH,
                              [(wv_img[:, 512 * c:512 * (c + 1)], "direct")]))

        conv_act_tok = []
        conv_store_tok = []
        for j, (src, ncol, dsts) in enumerate(conv_jobs):
            half = (j % 2) * D
            par = j % 2
            w = [(ac, conv_act_tok[j - 2])] if j >= 2 else []
            p.dma("sync", qc[par], wstage[:, half:half + ncol], src, waits=w)
            w2 = [(qc[par], qc[par].n)]
            if j >= 2:
                w2.append((gcv, conv_store_tok[j - 2]))
            tok = p.op("scalar", w2,
                       lambda e, a=wstageb[:, half:half + ncol],
                       b=wstage[:, half:half + ncol]: e.copy(out=a, in_=b),
                       bump=ac)
            conv_act_tok.append(tok)
            for dst, kindd in dsts:
                if kindd == "blk":
                    base, cch = dst
                    out_ap = mkap(base, 16384 * cch,
                                  [(128, 128), (262144, ND), (1, 128)])
                    p.dma("gpsimd", gcv, out_ap, wstageb[:, half:half + ncol],
                          waits=[(ac, tok)])
                elif kindd == "direct":
                    p.dma("gpsimd", gcv, dst, wstageb[:, half:half + ncol],
                          waits=[(ac, tok)])
                else:
                    dk = kindd[1]
                    p.dma("gpsimd", gcv, dst,
                          wstageb[:, half + 128 * dk:half + 128 * (dk + 1)],
                          waits=[(ac, tok)])
            conv_store_tok.append(gcv.n)
        conv_stores_full = (gcv, gcv.n)
        conv_acts = ac.n

        ph0.close()
        ph0_guard = conv_stores_full

        _bc_last = [None]

        def bcast_row(row, bank, dve_waits, pe_waits):
            b1 = bsp[0:1, 0:512]
            b2 = bsp[0:1, 512:1024]
            b3 = bsp[0:1, 1024:1536]
            t1 = tsp[0:1, 0:512]
            t2 = tsp[0:1, 512:1024]
            w0 = list(dve_waits)
            if _bc_last[0] is not None:
                w0.append((pe, _bc_last[0]))
            p.op("vector", w0, lambda e: e.tensor_copy(b1, row))
            p.op("vector", [], lambda e: e.tensor_tensor(
                t1, row, b1, op=AluOpType.subtract))
            p.op("vector", [], lambda e: e.tensor_copy(b2, t1))
            p.op("vector", [], lambda e: e.tensor_tensor(
                t2, t1, b2, op=AluOpType.subtract))
            dtok = p.op("vector", [], lambda e: e.tensor_copy(b3, t2), bump=dv)
            p.op("tensor", [(dv, dtok)] + pe_waits,
                 lambda e: e.matmul(ps[bank][:], ones1b[:], b1,
                                    start=True, stop=False))
            p.op("tensor", [], lambda e: e.matmul(ps[bank][:], ones1b[:], b2,
                                                  start=False, stop=False))
            tok = p.op("tensor", [],
                       lambda e: e.matmul(ps[bank][:], ones1b[:], b3,
                                          start=False, stop=True), bump=pe)
            _bc_last[0] = tok
            return tok

        if phases <= 0:
            phR.close()
            p.dma("gpsimd", gq, out_h[0:128, 0:128], maskt[:])
            _finish()
            return nc

        ph1 = contextlib.ExitStack()
        ln1b = ph1.enter_context(nc.sbuf_tensor("ln1b", [128, ND * 512], BF16))
        xst = ph1.enter_context(nc.sbuf_tensor("xst", [128, 4 * 512], F32))
        xbfst = ph1.enter_context(nc.sbuf_tensor("xbfst", [128, 2 * 512], BF16))
        sqst = ph1.enter_context(nc.sbuf_tensor("sqst", [128, 2 * 512], BF16))
        wqb = ph1.enter_context(nc.sbuf_tensor("wqb", [128, 2 * 2048], BF16))
        wvr = ph1.enter_context(nc.sbuf_tensor("wvr", [128, ND * KVH * DH], BF16))
        wkr = ph1.enter_context(nc.sbuf_tensor("wkr", [128, KVH * ND * 128], BF16))
        qraw = ph1.enter_context(nc.sbuf_tensor("qraw", [128, 2 * 512], BF16))
        scrA = ph1.enter_context(nc.sbuf_tensor("scrA", [128, 2 * 512], BF16))
        scrB = ph1.enter_context(nc.sbuf_tensor("scrB", [128, 2 * 512], BF16))
        stat = ph1.enter_context(nc.sbuf_tensor("stat", [1, 4 * 512], F32))
        ph2hold = contextlib.ExitStack()
        qav = ph2hold.enter_context(
            nc.sbuf_tensor("qav", [128, H * T], BF16, side="right"))
        kTr = ph2hold.enter_context(
            nc.sbuf_tensor("kTr", [128, KVH * T], BF16, side="right"))
        vtm = ph2hold.enter_context(
            nc.sbuf_tensor("vtm", [128, len(VTILES) * 512], BF16, side="right"))

        p.dma("sync", qkv, wvr[:], wv_img[:], waits=[ph0_guard])
        p.dma("sync", qkv, wkr[:], wk_img[:])
        kv_res_full = (qkv, qkv.n)

        prev_stat_pe = None
        prev_apply_dv = None
        prev_swap_pe = None
        prev_rope_dv = None
        prev_vev_ac = None
        stats_state = {}

        def emit_stats(tb):
            nonlocal_ = stats_state
            tcs = slice(512 * tb, 512 * (tb + 1))
            stat_mm_tok = []
            prev = nonlocal_.get("prev")
            for c in range(ND):
                buf = c % 2
                if c >= 2:
                    w = [(pe, stat_mm_tok[c - 2])]
                elif tb == 0:
                    w = [ph0_guard]
                else:
                    w = [(pe, prev)]
                p.dma("sync", qx[buf], xst[:, 512 * buf:512 * (buf + 1)],
                      xT[128 * c:128 * (c + 1), tcs], waits=w)
                xx = xst[:, 512 * buf:512 * (buf + 1)]
                xb = xbfst[:, 512 * buf:512 * (buf + 1)]
                sb = sqst[:, 512 * buf:512 * (buf + 1)]
                w = [(qx[buf], qx[buf].n)]
                if c >= 2:
                    w.append((pe, stat_mm_tok[c - 2]))
                elif tb > 0:
                    w.append((pe, prev))
                p.op("vector", w, lambda e, xb=xb, xx=xx: e.tensor_copy(xb, xx))
                tok = p.op("vector", [],
                           lambda e, sb=sb, xb=xb: e.tensor_tensor(
                               sb, xb, xb, op=AluOpType.mult), bump=dv)
                w2 = [(dv, tok)]
                if c == 0:
                    w2 += [free[0], free[1], (pl, tok_ones)]
                p.op("tensor", w2,
                     lambda e, xb=xb, c=c: e.matmul(
                         ps[0][0:1, :], onesb[:], xb,
                         start=(c == 0), stop=(c == ND - 1)))
                tok = p.op("tensor", [],
                           lambda e, sb=sb, c=c: e.matmul(
                               ps[1][0:1, :], onesb[:], sb,
                               start=(c == 0), stop=(c == ND - 1)), bump=pe)
                stat_mm_tok.append(tok)
            nonlocal_["prev"] = stat_mm_tok[-1]
            nonlocal_["pe"] = pe.n

        emit_stats(0)
        for tb in range(tbs):
            tcs = slice(512 * tb, 512 * (tb + 1))
            stats_pe = stats_state["pe"]
            mu_s = stat[0:1, 0:512]
            m2_s = stat[0:1, 512:1024]
            sg_s = stat[0:1, 1024:1536]
            r_s = stat[0:1, 1536:2048]
            p.op("vector", [(pe, stats_pe)],
                 lambda e, mu_s=mu_s: e.tensor_scalar(
                     mu_s, ps[0][0:1, :], 1.0 / D, None,
                     op0=AluOpType.mult, op1=AluOpType.bypass))
            p.op("vector", [],
                 lambda e, m2_s=m2_s: e.tensor_scalar(
                     m2_s, ps[1][0:1, :], 1.0 / D, None,
                     op0=AluOpType.mult, op1=AluOpType.bypass))
            p.op("vector", [],
                 lambda e, sg_s=sg_s, mu_s=mu_s: e.tensor_tensor(
                     sg_s, mu_s, mu_s, op=AluOpType.mult))
            tok = p.op("vector", [],
                       lambda e, m2_s=m2_s, sg_s=sg_s: e.scalar_tensor_tensor(
                           m2_s, m2_s, EPS, sg_s,
                           op0=AluOpType.add, op1=AluOpType.subtract), bump=dv)
            free[0] = free[1] = (dv, tok)
            tok = p.op("scalar", [(dv, tok)],
                       lambda e, sg_s=sg_s, m2_s=m2_s: e.activation(
                           out=sg_s, in_=m2_s, func=AF.Sqrt, scale=1.0),
                       bump=ac)
            rtok = p.op("vector", [(ac, tok)],
                        lambda e, r_s=r_s, sg_s=sg_s: e.reciprocal(r_s, sg_s),
                        bump=dv)

            bcast_row(mu_s, 2, [(dv, rtok)],
                      [(pl, tok_ones), free[2], free[3]])
            bc_tok = bcast_row(r_s, 3, [], [])

            apply_tok = []
            for c in range(ND):
                buf = 2 + c % 2
                if c >= 2:
                    w = [(dv, apply_tok[c - 2])]
                elif tb > 0:
                    w = [(dv, prev_apply_dv)]
                else:
                    w = []
                p.dma("sync", qa[c % 2], xst[:, 512 * buf:512 * (buf + 1)],
                      xT[128 * c:128 * (c + 1), tcs], waits=w)
                xx = xst[:, 512 * buf:512 * (buf + 1)]
                w2 = [(qa[c % 2], qa[c % 2].n)]
                if c == 0:
                    w2.append((pe, bc_tok))
                p.op("vector", w2,
                     lambda e, xx=xx: e.tensor_tensor(xx, xx, ps[2][:],
                                                      op=AluOpType.subtract))
                tok = p.op("vector", [],
                           lambda e, xx=xx, dst=ln1b[:, 512 * c:512 * (c + 1)]:
                           e.tensor_tensor(dst, xx, ps[3][:], op=AluOpType.mult),
                           bump=dv)
                apply_tok.append(tok)
            prev_apply_dv = apply_tok[-1]
            apply_dv = dv.n
            free[2] = free[3] = (dv, apply_dv)

            qk_jobs = [("q", i) for i in range(ND)] + [("k", i) for i in range(KVH)]
            evac_tok = []
            rope_tok = []
            swap_tok = []
            wq_ld = []
            for j, (kind, di) in enumerate(qk_jobs):
                par = j % 2
                if kind == "q":
                    if j >= 2:
                        w = [(pe, swap_tok[j - 2])]
                    elif tb > 0:
                        w = [(pe, prev_swap_pe)]
                    else:
                        w = [conv_stores_full]
                    p.dma("sync", qw[par], wqb[:, 2048 * par:2048 * par + 1024],
                          mkap(wq_bf, di * 262144,
                               [(128, 128), (16384, 8), (1, 128)]), waits=w)
                    p.dma("sync", qw[par],
                          wqb[:, 2048 * par + 1024:2048 * par + 2048],
                          mkap(wq_bf, di * 262144 + 8 * 16384,
                               [(128, 128), (16384, 8), (1, 128)]))
                    wq_ld.append((qw[par], qw[par].n))
                else:
                    wq_ld.append(None)

                qps = ps[4 + par][:]
                w = [(dv, apply_dv)]
                if kind == "q":
                    w.append(wq_ld[j])
                else:
                    w.append(kv_res_full)
                if j >= 2:
                    w.append((ac, evac_tok[j - 2]))
                elif tb > 0:
                    w.append((ac, prev_vev_ac))
                for c in range(ND):
                    if kind == "q":
                        lhs = wqb[:, 2048 * par + 128 * c:2048 * par + 128 * (c + 1)]
                    else:
                        lhs = wkr[:, di * 2048 + 128 * c:di * 2048 + 128 * (c + 1)]
                    tok = p.op("tensor", w if c == 0 else [],
                               lambda e, lhs=lhs, qps=qps, c=c,
                               rhs=ln1b[:, 512 * c:512 * (c + 1)]: e.matmul(
                                   qps, lhs, rhs,
                                   start=(c == 0), stop=(c == ND - 1)),
                               bump=(pe if c == ND - 1 else None))
                mm_tok = tok
                qr = qraw[:, 512 * par:512 * (par + 1)]
                w = [(pe, mm_tok)]
                if j >= 2:
                    w += [(pe, swap_tok[j - 2]), (dv, rope_tok[j - 2])]
                elif tb > 0:
                    w += [(pe, prev_swap_pe), (dv, prev_rope_dv)]
                tok = p.op("scalar", w,
                           lambda e, qr=qr, qps=qps,
                           s=(QSCALE if kind == "q" else 1.0):
                           e.activation(out=qr, in_=qps, func=AF.Copy, scale=s),
                           bump=ac)
                evac_tok.append(tok)
                w = [(ac, tok)]
                if j >= 1:
                    w.append((dv, rope_tok[j - 1]))
                elif tb > 0:
                    w.append((dv, prev_rope_dv))
                stok = p.op("tensor", w,
                            lambda e, qr=qr: e.matmul(ps[6][:, 0:512], permP[:],
                                                      qr, start=True, stop=True),
                            bump=pe)
                swap_tok.append(stok)
                if kind == "q":
                    dst = qav[:, 2048 * di + 512 * tb:2048 * di + 512 * (tb + 1)]
                else:
                    dst = kTr[:, 2048 * di + 512 * tb:2048 * di + 512 * (tb + 1)]
                sA = scrA[:, 512 * par:512 * (par + 1)]
                sB = scrB[:, 512 * par:512 * (par + 1)]
                w = [(ac, tok)]
                if tb == 0 and j == 0:
                    w.append(ph0_guard)
                    w.append(consts_full)
                p.op("vector", w,
                     lambda e, sA=sA, qr=qr, cs=cosE[:, tcs]: e.tensor_tensor(
                         sA, qr, cs, op=AluOpType.mult))
                p.op("vector", [(pe, stok)],
                     lambda e, sB=sB, sn=sinE[:, tcs]: e.tensor_tensor(
                         sB, ps[6][:, 0:512], sn, op=AluOpType.mult))
                tok = p.op("vector", [],
                           lambda e, dst=dst, sA=sA, sB=sB: e.tensor_tensor(
                               dst, sA, sB, op=AluOpType.add), bump=dv)
                rope_tok.append(tok)
            prev_swap_pe = swap_tok[-1]
            prev_rope_dv = rope_tok[-1]
            free[4] = (ac, evac_tok[-2])
            free[5] = (ac, evac_tok[-1])

            if tb + 1 < tbs:
                emit_stats(tb + 1)

            vts = [t_ for t_ in VTILES if 512 * tb <= 128 * t_ < 512 * (tb + 1)]
            vev_tok = []
            for i, vt in enumerate(vts):
                par = i % 2
                vps = ps[4 + par][:]
                w = [(dv, apply_dv), kv_res_full]
                if i >= 2:
                    w.append((ac, vev_tok[i - 2]))
                else:
                    w.append(free[4 + par])
                col0 = 128 * vt - 512 * tb
                for c in range(ND):
                    lhs = ln1b[:, 512 * c + col0:512 * c + col0 + 128]
                    tok = p.op("tensor", w if c == 0 else [],
                               lambda e, lhs=lhs, vps=vps, c=c,
                               rhs=wvr[:, 512 * c:512 * (c + 1)]: e.matmul(
                                   vps, lhs, rhs,
                                   start=(c == 0), stop=(c == ND - 1)),
                               bump=(pe if c == ND - 1 else None))
                dst = vtm[:, 512 * VSLOT[vt]:512 * (VSLOT[vt] + 1)]
                w2 = [(pe, tok)]
                if tb == 0 and i == 0:
                    w2.append(ph0_guard)
                tok = p.op("scalar", w2,
                           lambda e, dst=dst, vps=vps: e.activation(
                               out=dst, in_=vps, func=AF.Copy, scale=1.0),
                           bump=ac)
                vev_tok.append(tok)
            for i in range(max(0, len(vts) - 2), len(vts)):
                free[4 + i % 2] = (ac, vev_tok[i])
            prev_vev_ac = vev_tok[-1] if vev_tok else prev_vev_ac

        ph1_pe = pe.n
        ph1_ac = ac.n
        ph1.close()
        phR.close()

        if taps:
            p.dma("gpsimd", gq, qav_o[:], qav[:],
                  waits=[(pe, pe.n), (dv, dv.n), (ac, ac.n)])
            p.dma("gpsimd", gq, kTr_o[:], kTr[:])
            p.dma("gpsimd", gq, vtm_o[:], vtm[:])
            p.dma("gpsimd", gq, ln1_o[:], ln1b[:])

        if phases <= 1:
            ph2hold.close()
            p.dma("gpsimd", gq, out_h[0:128, 0:128], maskt[:])
            _finish()
            return nc

        ph2 = contextlib.ExitStack()
        attnq = ph2.enter_context(nc.sbuf_tensor("attnq", [128, 4 * TK], F32))
        attnT = ph2.enter_context(nc.sbuf_tensor("attnT", [128, NKT * 512], BF16))
        sums = ph2.enter_context(nc.sbuf_tensor("sums", [128, 16], F32))
        rsum = ph2.enter_context(nc.sbuf_tensor("rsum", [128, 8], F32))
        midx = ph2.enter_context(nc.sbuf_tensor("midx", [128, 2 * 512], I32))
        mbf = ph2.enter_context(nc.sbuf_tensor("mbf", [128, 512], BF16))
        mpp = ph2.enter_context(nc.sbuf_tensor("mpp", [128, 512], BF16))
        mac2 = ph2.enter_context(nc.sbuf_tensor("mac2", [128, 2 * 512], BF16))

        mat_jobs = []
        for s_ in range(2):
            mat_jobs.append((gsh[128 * s_:128 * (s_ + 1), :], INTER, ccg_in[s_], True))
        for s_ in range(2):
            mat_jobs.append((ush[128 * s_:128 * (s_ + 1), :], INTER, ccu_in[s_], True))
        for s_ in range(DSLOTS):
            mat_jobs.append((dsh[128 * s_:128 * (s_ + 1), :], D, ccd_in[s_], False))
        mat_pieces = []
        for j, (srcj, wj, dstj, isgu) in enumerate(mat_jobs):
            for b0 in range(0, wj // 128, 4):
                b1 = min(wj // 128, b0 + 4)
                mat_pieces.append((srcj, dstj, isgu, b0, b1))
        mat_state = {"i": 0, "dv": [], "st": []}
        ph1_dv = dv.n

        def emit_mat_piece():
            i = mat_state["i"]
            if i >= len(mat_pieces):
                return
            mat_state["i"] += 1
            srcj, dstj, isgu, b0, b1 = mat_pieces[i]
            ncol = 128 * (b1 - b0)
            par = i % 2
            w = []
            if i < 2:
                w += [(pe, ph1_pe), (ac, ph1_ac), (dv, ph1_dv)]
            else:
                w.append((dv, mat_state["dv"][i - 2]))
            p.dma("sync", qis[par], midx[:, 512 * par:512 * par + ncol],
                  srcj[:, 128 * b0:128 * b1], waits=w)
            ib = mbf[:, 0:ncol]
            acc = mac2[:, 512 * par:512 * par + ncol]
            ltab = lutsD if isgu else luts
            w2 = [(qis[par], qis[par].n), (dv, lut_prep)]
            if i >= 2:
                w2.append((gmt, mat_state["st"][i - 2]))
            p.op("vector", w2,
                 lambda e, ib=ib, b=midx[:, 512 * par:512 * par + ncol]:
                 e.tensor_copy(ib, b))
            p.op("vector", [],
                 lambda e, acc=acc, ib=ib, ltab=ltab: e.tensor_scalar(
                     acc, ib, 0.0, ltab[:, 0:1],
                     op0=AluOpType.is_equal, op1=AluOpType.mult))
            tok = None
            for t in range(1, 16):
                p.op("vector", [],
                     lambda e, ib=ib, t=t, mpv=mpp[:, 0:ncol], ltab=ltab:
                     e.tensor_scalar(mpv, ib, float(t), ltab[:, t:t + 1],
                                     op0=AluOpType.is_equal, op1=AluOpType.mult))
                tok = p.op("vector", [],
                           lambda e, acc=acc, mpv=mpp[:, 0:ncol]: e.tensor_tensor(
                               acc, acc, mpv, op=AluOpType.add),
                           bump=(dv if t == 15 else None))
            mat_state["dv"].append(tok)
            out_ap = mkap(dstj, 16384 * b0,
                          [(128, 128), (16384, b1 - b0), (1, 128)])
            p.dma("gpsimd", gmt, out_ap, acc, waits=[(dv, tok)])
            mat_state["st"].append(gmt.n)
            done = mat_state["i"]
            for nth, (npieces, cin, cout) in enumerate(
                    [(2 * (INTER // 512 + 1), ccg_in, ccg),
                     (4 * (INTER // 512 + 1), ccu_in, ccu),
                     (len(mat_pieces), ccd_in, ccd)]):
                if done == npieces and cc.n == nth:
                    p.op("gpsimd", [(gmt, gmt.n)],
                         lambda e, cin=cin, cout=cout: e.collective_compute(
                             "AllGather", AluOpType.bypass,
                             ins=[cin[:]], outs=[cout[:]],
                             replica_groups=[list(range(8))]), bump=cc)

        last_exp = {i: None for i in range(6)}
        last_trT = {0: None, 1: None}
        avfree = None
        prev_transposes = None
        prev_scale = {}
        first_ph2 = True
        for h in range(H):
            kv = h // 4
            for qb in range(4):
                qts = [4 * qb + i for i in range(4)]
                memset_tok = None
                if qb < 2:
                    for i, qi in enumerate(qts):
                        kcols = TK if qi >= 9 else 128 * (qi + 1)
                        if kcols < TK:
                            w = []
                            if prev_transposes is not None:
                                w.append((pe, prev_transposes))
                            if first_ph2:
                                w.append((pe, ph1_pe))
                            memset_tok = p.op(
                                "vector", w,
                                lambda e, a=attnq[:, TK * i + kcols:TK * (i + 1)]:
                                e.memset(a, 0.0), bump=dv)
                exp_last = {}
                for i, qi in enumerate(qts):
                    kcols = TK if qi >= 9 else 128 * (qi + 1)
                    nseg = (kcols + 511) // 512
                    bank0 = 3 * (i % 2)
                    lhs = qav[:, 2048 * h + 128 * qi:2048 * h + 128 * (qi + 1)]
                    segs = []
                    for sg_ in range(nseg):
                        c0 = 512 * sg_
                        c1 = min(kcols, c0 + 512)
                        pieces = []
                        if c0 < 128:
                            pieces.append((c0, min(c1, 128), 2048 * kv + c0))
                        if c1 > 128:
                            w0 = max(c0, 128)
                            pieces.append((w0, c1, 2048 * kv + 1024 + (w0 - 128)))
                        bank = bank0 + sg_
                        w = []
                        if last_exp[bank] is not None:
                            w.append(last_exp[bank])
                        if first_ph2:
                            w.append((pe, ph1_pe))
                            w.append((ac, ph1_ac))
                        first = True
                        for (a0, a1, kc) in pieces:
                            tok = p.op("tensor", w if first else [],
                                       lambda e, bank=bank, a0=a0, a1=a1, kc=kc,
                                       c0=c0, lhs=lhs: e.matmul(
                                           ps[bank][:, a0 - c0:a1 - c0], lhs,
                                           kTr[:, kc:kc + (a1 - a0)],
                                           start=True, stop=True),
                                       bump=pe)
                            first = False
                        segs.append((bank, c0, c1, tok))
                    mask_dv = None
                    if qi <= 8:
                        bank, c0, c1, mmtok = segs[-1]
                        off = (kcols - 128) - c0
                        mask_dv = p.op("vector", [(pe, mmtok)],
                                       lambda e, bank=bank, off=off:
                                       e.tensor_tensor(
                                           ps[bank][:, off:off + 128],
                                           ps[bank][:, off:off + 128],
                                           maskt[:], op=AluOpType.add), bump=dv)
                    nparts = 0
                    for (bank, c0, c1, mmtok) in segs:
                        w = [(pe, mmtok)]
                        if mask_dv is not None and bank == segs[-1][0]:
                            w.append((dv, mask_dv))
                        if memset_tok is not None:
                            w.append((dv, memset_tok))
                        if nparts == 0 and i in prev_scale:
                            w.append((dv, prev_scale[i]))
                        scol = 4 * i + nparts
                        tok = p.op("scalar", w,
                                   lambda e, bank=bank, c0=c0, c1=c1, i=i,
                                   scol=scol: e.activation(
                                       out=attnq[:, TK * i + c0:TK * i + c1],
                                       in_=ps[bank][:, 0:c1 - c0], func=AF.Exp,
                                       scale=1.0,
                                       accum_out=sums[:, scol:scol + 1]),
                                   bump=ac)
                        last_exp[bank] = (ac, tok)
                        nparts += 1
                    exp_last[i] = (tok, nparts)
                first_ph2 = False
                if ph2l < 2:
                    continue
                scale_tok = {}
                for i, qi in enumerate(qts):
                    tok, nparts = exp_last[i]
                    p.op("vector", [(ac, tok)],
                         lambda e, i=i, c0=4 * i, nc_=nparts: e.tensor_reduce(
                             rsum[:, i:i + 1], sums[:, c0:c0 + nc_],
                             axis=mybir.AxisListType.X, op=AluOpType.add))
                for i in range(4):
                    p.op("vector", [],
                         lambda e, i=i: e.reciprocal(rsum[:, 4 + i:5 + i],
                                                     rsum[:, i:i + 1]))
                for i, qi in enumerate(qts):
                    kcols = TK if qi >= 9 else 128 * (qi + 1)
                    stok = p.op("vector", [],
                                lambda e, i=i, kcols=kcols: e.tensor_scalar(
                                    attnq[:, TK * i:TK * i + kcols],
                                    attnq[:, TK * i:TK * i + kcols],
                                    rsum[:, 4 + i:5 + i], None,
                                    op0=AluOpType.mult, op1=AluOpType.bypass),
                                bump=dv)
                    scale_tok[i] = stok
                    prev_scale[i] = stok
                if ph2l < 3:
                    continue
                mmax = min(NKT - 1, 4 * qb + 3)
                tr_ev = []
                ac_bar = ac.n
                for m in range(mmax + 1):
                    bT = 4 + m % 2
                    for i in range(4):
                        w = [(dv, scale_tok[i])]
                        if i == 0:
                            if m == 0:
                                w.append((ac, ac_bar))
                            if last_trT[m % 2] is not None:
                                w.append(last_trT[m % 2])
                        tok = p.op("tensor", w,
                                   lambda e, m=m, i=i, bT=bT: e.transpose(
                                       ps[bT][:, 128 * i:128 * (i + 1)],
                                       attnq[:, TK * i + 128 * m:
                                             TK * i + 128 * (m + 1)],
                                       identf[:]),
                                   bump=(pe if i == 3 else None))
                    etok = p.op("scalar", [(pe, tok)],
                                lambda e, m=m, bT=bT: e.activation(
                                    out=attnT[:, 512 * m:512 * (m + 1)],
                                    in_=ps[bT][:, 0:512],
                                    func=AF.Copy, scale=1.0), bump=ac)
                    last_trT[m % 2] = (ac, etok)
                    last_exp[bT] = (ac, etok)
                    tr_ev.append(etok)
                prev_transposes = tok
                if ph2l < 4:
                    continue
                for m in range(mmax + 1):
                    vt = 0 if m == 0 else m + 7
                    lhs = vtm[:, 512 * VSLOT[vt] + 128 * kv:
                              512 * VSLOT[vt] + 128 * (kv + 1)]
                    w = [(ac, tr_ev[m])]
                    if m == 0 and avfree is not None:
                        w.append(avfree)
                    tok = p.op("tensor", w,
                               lambda e, lhs=lhs, m=m, mmax=mmax: e.matmul(
                                   ps[6][:, 0:512], lhs,
                                   attnT[:, 512 * m:512 * (m + 1)],
                                   start=(m == 0), stop=(m == mmax)),
                               bump=(pe if m == mmax else None))
                dst = qav[:, 2048 * h + 512 * qb:2048 * h + 512 * (qb + 1)]
                tok = p.op("scalar", [(pe, tok)],
                           lambda e, dst=dst: e.activation(
                               out=dst, in_=ps[6][:, 0:512], func=AF.Copy,
                               scale=1.0), bump=ac)
                avfree = (ac, tok)
                if mat_state["i"] < 56:
                    emit_mat_piece()
        mat_spill = max(0, len(mat_pieces) - mat_state["i"]) 
        ag_done = 3
        attn_ac = ac.n
        ph2.close()

        if taps:
            p.dma("gpsimd", gq, av_o[:], qav[:],
                  waits=[(pe, pe.n), (dv, dv.n), (ac, ac.n)])
            p.dma("gpsimd", gq, aq_o[:], attnq[:])
            p.dma("gpsimd", gq, aT_o[:], attnT[:])
            p.dma("gpsimd", gq, sums_o[:], sums[:])
            p.dma("gpsimd", gq, rsum_o[:], rsum[:])
            sct = p.op("scalar", [(pe, pe.n), (ac, ac.n)],
                       lambda e: e.activation(out=scdbg[:], in_=ps[3][:],
                                              func=AF.Copy, scale=1.0), bump=ac)
            p.dma("gpsimd", gq, sc_o[:], scdbg[:], waits=[(ac, sct)])

        if phases <= 2:
            ph2hold.close()
            p.dma("gpsimd", gq, out_h[0:128, 0:128], maskt[:])
            _finish()
            return nc

        ph3 = contextlib.ExitStack()
        wob = ph3.enter_context(nc.sbuf_tensor("wob", [128, 2 * 2048], BF16))
        xo = ph3.enter_context(nc.sbuf_tensor("xo", [128, 2 * 512], F32))
        x2st = ph3.enter_context(nc.sbuf_tensor("x2st", [128, 2 * 512], BF16))
        x2a = ph3.enter_context(nc.sbuf_tensor("x2a", [128, 2 * 512], BF16))
        x2sq = ph3.enter_context(nc.sbuf_tensor("x2sq", [128, 2 * 512], BF16))
        st4 = ph3.enter_context(nc.sbuf_tensor("st4", [1, 1024], F32))

        ofree = {0: None, 1: None}
        x2_store_tok = {0: None, 1: None}
        wo_free = {0: None, 1: None}
        prev_smm = None
        for tc in range(4):
            dt_mm_tok = {}
            tc_store_tok = []
            for dt in range(ND):
                bk = dt % 2
                w = [(ac, attn_ac)] if (tc == 0 and dt < 2) else []
                if wo_free[bk] is not None:
                    w.append(wo_free[bk])
                p.dma("sync", qw[bk], wob[:, 2048 * bk:2048 * bk + 1024],
                      mkap(wo_bf, dt * 262144,
                           [(128, 128), (16384, 8), (1, 128)]), waits=w)
                p.dma("sync", qw[bk], wob[:, 2048 * bk + 1024:2048 * bk + 2048],
                      mkap(wo_bf, dt * 262144 + 8 * 16384,
                           [(128, 128), (16384, 8), (1, 128)]))
                wo_ld = (qw[bk], qw[bk].n)
                pso = ps[4 + bk][:]
                xbuf = xo[:, 512 * bk:512 * (bk + 1)]
                x2b = x2st[:, 512 * bk:512 * (bk + 1)]
                w_ld = [ofree[bk]] if ofree[bk] else [(ac, attn_ac)]
                p.dma("gpsimd", gx[bk], xbuf,
                      xT[128 * dt:128 * (dt + 1), 512 * tc:512 * (tc + 1)],
                      waits=w_ld)
                xld = (gx[bk], gx[bk].n)
                w = [wo_ld]
                if ofree[bk] is not None:
                    w.append(ofree[bk])
                else:
                    w += [free[4 + bk], (ac, attn_ac)]
                for hh in range(H):
                    tok = p.op("tensor", w if hh == 0 else [],
                               lambda e, pso=pso, hh=hh, bk=bk, tc=tc:
                               e.matmul(
                                   pso,
                                   wob[:, 2048 * bk + 128 * hh:
                                       2048 * bk + 128 * (hh + 1)],
                                   qav[:, 2048 * hh + 512 * tc:
                                       2048 * hh + 512 * (tc + 1)],
                                   start=(hh == 0), stop=(hh == H - 1)),
                               bump=(pe if hh == H - 1 else None))
                dt_mm_tok[dt] = tok
                wo_free[bk] = (pe, tok)
                wv_ = [(pe, tok), xld]
                if x2_store_tok[bk] is not None:
                    wv_.append(x2_store_tok[bk])
                dtok = p.op("vector", wv_,
                            lambda e, x2b=x2b, xbuf=xbuf, pso=pso:
                            e.tensor_tensor(x2b, xbuf, pso, op=AluOpType.add),
                            bump=dv)
                ofree[bk] = (dv, dtok)
                p.dma("gpsimd", gst[bk],
                      x2h[128 * dt:128 * (dt + 1), 512 * tc:512 * (tc + 1)],
                      x2b, waits=[(dv, dtok)])
                x2_store_tok[bk] = (gst[bk], gst[bk].n)
                tc_store_tok.append(x2_store_tok[bk])
            smm_tok = []
            for dt in range(ND):
                buf = dt % 2
                w = [tc_store_tok[dt]]
                if smm_tok[:max(0, dt - 1)]:
                    pass
                if dt >= 2:
                    w.append((pe, smm_tok[dt - 2]))
                elif prev_smm is not None:
                    w.append((pe, prev_smm))
                p.dma("gpsimd", gx[buf], x2a[:, 512 * buf:512 * (buf + 1)],
                      x2h[128 * dt:128 * (dt + 1), 512 * tc:512 * (tc + 1)],
                      waits=w)
                ld = (gx[buf], gx[buf].n)
                xb = x2a[:, 512 * buf:512 * (buf + 1)]
                sb = x2sq[:, 512 * buf:512 * (buf + 1)]
                tok = p.op("vector", [ld],
                           lambda e, sb=sb, xb=xb: e.tensor_tensor(
                               sb, xb, xb, op=AluOpType.mult), bump=dv)
                w2 = [(dv, tok)]
                if dt == 0:
                    w2 += [free[0], free[1], (pl, tok_ones)]
                p.op("tensor", w2,
                     lambda e, xb=xb, dt=dt: e.matmul(
                         ps[0][0:1, :], onesb[:], xb,
                         start=(dt == 0), stop=(dt == ND - 1)))
                tok = p.op("tensor", [],
                           lambda e, sb=sb, dt=dt: e.matmul(
                               ps[1][0:1, :], onesb[:], sb,
                               start=(dt == 0), stop=(dt == ND - 1)), bump=pe)
                smm_tok.append(tok)
            prev_smm = smm_tok[-1]
            stats_pe = pe.n
            mu_s = mu2r[0:1, 512 * tc:512 * (tc + 1)]
            r_s = mu2r[0:1, T + 512 * tc:T + 512 * (tc + 1)]
            m2_s = st4[0:1, 0:512]
            sg_s = st4[0:1, 512:1024]
            p.op("vector", [(pe, stats_pe)],
                 lambda e, mu_s=mu_s: e.tensor_scalar(
                     mu_s, ps[0][0:1, :], 1.0 / D, None,
                     op0=AluOpType.mult, op1=AluOpType.bypass))
            p.op("vector", [],
                 lambda e, m2_s=m2_s: e.tensor_scalar(
                     m2_s, ps[1][0:1, :], 1.0 / D, None,
                     op0=AluOpType.mult, op1=AluOpType.bypass))
            p.op("vector", [],
                 lambda e, sg_s=sg_s, mu_s=mu_s: e.tensor_tensor(
                     sg_s, mu_s, mu_s, op=AluOpType.mult))
            tok = p.op("vector", [],
                       lambda e, m2_s=m2_s, sg_s=sg_s: e.scalar_tensor_tensor(
                           m2_s, m2_s, EPS, sg_s,
                           op0=AluOpType.add, op1=AluOpType.subtract), bump=dv)
            free[0] = free[1] = (dv, tok)
            tok = p.op("scalar", [(dv, tok)],
                       lambda e, sg_s=sg_s, m2_s=m2_s: e.activation(
                           out=sg_s, in_=m2_s, func=AF.Sqrt, scale=1.0),
                       bump=ac)
            p.op("vector", [(ac, tok)],
                 lambda e, r_s=r_s, sg_s=sg_s: e.reciprocal(r_s, sg_s), bump=dv)
            for _ in range(4):
                emit_mat_piece()
        while mat_state["i"] < len(mat_pieces):
            emit_mat_piece()
        x2_done = [(gst[0], gst[0].n), (gst[1], gst[1].n)]
        free[4] = ofree[0]
        free[5] = ofree[1]
        oproj_pe = pe.n
        oproj_dv = dv.n
        ln2_rows_dv = dv.n
        ph3.close()
        ph2hold.close()

        ph5 = contextlib.ExitStack()
        hbuf = ph5.enter_context(nc.sbuf_tensor("hbuf", [128, NI * 1024], BF16))
        ln2b = ph5.enter_context(nc.sbuf_tensor("ln2b", [128, ND * 1024], BF16))
        wgb = ph5.enter_context(nc.sbuf_tensor("wgb", [128, 2 * 2048], BF16))
        wub = ph5.enter_context(nc.sbuf_tensor("wub", [128, 2 * 2048], BF16))
        wdb = ph5.enter_context(nc.sbuf_tensor("wdb", [128, 2 * NI * 128], BF16))
        x2f = ph5.enter_context(nc.sbuf_tensor("x2f", [128, 2 * 512], BF16))
        sgs = ph5.enter_context(nc.sbuf_tensor("sgs", [128, 2 * 512], BF16))
        lnf = ph5.enter_context(nc.sbuf_tensor("lnf", [128, 2 * 512], F32))
        outs = ph5.enter_context(nc.sbuf_tensor("outs", [128, 2 * 512], F32))

        first_ffn_ld = True
        dbg_half = halves
        out_store_tok = {0: None, 1: None}
        sil_tok = {0: None, 1: None}
        hmul_tok = {0: None, 1: None}
        wg_free = {0: None, 1: None}
        wd_free = {0: None, 1: None}
        dn_free = {0: None, 1: None}
        for half in range(dbg_half):
            x2f_guard = (dv, dv.n)
            apply_dv_tok = []
            for tcl in range(2):
                tc = 2 * half + tcl
                mu_s = mu2r[0:1, 512 * tc:512 * (tc + 1)]
                r_s = mu2r[0:1, T + 512 * tc:T + 512 * (tc + 1)]
                bcast_row(mu_s, 2, [(dv, ln2_rows_dv)],
                          [(pl, tok_ones), free[2], free[3]])
                bc_tok = bcast_row(r_s, 3, [], [])
                for dt in range(ND):
                    buf = dt % 2
                    if len(apply_dv_tok) >= 2:
                        w = [(dv, apply_dv_tok[-2])]
                    else:
                        w = [x2f_guard]
                    p.dma("gpsimd", gx[buf], x2f[:, 512 * buf:512 * (buf + 1)],
                          x2h[128 * dt:128 * (dt + 1), 512 * tc:512 * (tc + 1)],
                          waits=w)
                    xb = x2f[:, 512 * buf:512 * (buf + 1)]
                    w2 = [(gx[buf], gx[buf].n)]
                    if dt == 0:
                        w2.append((pe, bc_tok))
                    dst = ln2b[:, 1024 * dt + 512 * tcl:1024 * dt + 512 * (tcl + 1)]
                    lf = lnf[:, 512 * buf:512 * (buf + 1)]
                    p.op("vector", w2,
                         lambda e, lf=lf, xb=xb: e.tensor_tensor(
                             lf, xb, ps[2][:], op=AluOpType.subtract))
                    tok = p.op("vector", [],
                               lambda e, dst=dst, lf=lf: e.tensor_tensor(
                                   dst, lf, ps[3][:], op=AluOpType.mult),
                               bump=dv)
                    apply_dv_tok.append(tok)
                free[2] = free[3] = (dv, apply_dv_tok[-1])
            ln2_apply_dv = dv.n

            for it in range(NI):
                par = it % 2
                w = [(cc, 1)]
                if first_ffn_ld:
                    w += [(pe, oproj_pe)]
                    first_ffn_ld = False
                if wg_free[par] is not None:
                    w.append(wg_free[par])
                cst = NI * 16384
                p.dma("sync", qg[par], wgb[:, 2048 * par:2048 * par + 1024],
                      mkap(ccg, it * 16384,
                           [(128, 128), (cst, 8), (1, 128)]), waits=w)
                p.dma("sync", qg[par], wgb[:, 2048 * par + 1024:2048 * (par + 1)],
                      mkap(ccg, it * 16384 + 8 * cst,
                           [(128, 128), (cst, 8), (1, 128)]))
                p.dma("sync", qu[par], wub[:, 2048 * par:2048 * par + 1024],
                      mkap(ccu, it * 16384,
                           [(128, 128), (cst, 8), (1, 128)]),
                      waits=([(cc, 2)] if (half == 0 and it < 2) else []))
                p.dma("sync", qu[par], wub[:, 2048 * par + 1024:2048 * (par + 1)],
                      mkap(ccu, it * 16384 + 8 * cst,
                           [(128, 128), (cst, 8), (1, 128)]))
                wld = [(qg[par], qg[par].n), (qu[par], qu[par].n)]
                gu_tok = None
                for tcl in range(2):
                    gps = ps[par][:]
                    ups = ps[4 + par][:]
                    w = list(wld)
                    if it < 2 and tcl == 0:
                        w.append((dv, ln2_apply_dv))
                        w += [free[par], free[4 + par]]
                    if sil_tok[par] is not None:
                        w.append(sil_tok[par])
                    if hmul_tok[par] is not None:
                        w.append(hmul_tok[par])
                    for c in range(ND):
                        p.op("tensor", w if c == 0 else [],
                             lambda e, gps=gps, par=par, c=c, tcl=tcl: e.matmul(
                                 gps,
                                 wgb[:, 2048 * par + 128 * c:
                                     2048 * par + 128 * (c + 1)],
                                 ln2b[:, 1024 * c + 512 * tcl:
                                      1024 * c + 512 * (tcl + 1)],
                                 start=(c == 0), stop=(c == ND - 1)))
                        tok = p.op("tensor", [],
                                   lambda e, ups=ups, par=par, c=c, tcl=tcl:
                                   e.matmul(
                                       ups,
                                       wub[:, 2048 * par + 128 * c:
                                           2048 * par + 128 * (c + 1)],
                                       ln2b[:, 1024 * c + 512 * tcl:
                                            1024 * c + 512 * (tcl + 1)],
                                       start=(c == 0), stop=(c == ND - 1)),
                                   bump=(pe if c == ND - 1 else None))
                    gu_tok = tok
                    sg_b = sgs[:, 512 * par:512 * (par + 1)]
                    w3 = [(pe, gu_tok)]
                    if hmul_tok[par] is not None:
                        w3.append(hmul_tok[par])
                    stok = p.op("scalar", w3,
                                lambda e, sg_b=sg_b, gps=gps: e.activation(
                                    out=sg_b, in_=gps, func=AF.Silu, scale=1.0),
                                bump=ac)
                    sil_tok[par] = (ac, stok)
                    dst = hbuf[:, 1024 * it + 512 * tcl:1024 * it + 512 * (tcl + 1)]
                    htok = p.op("vector", [(ac, stok)],
                                lambda e, dst=dst, sg_b=sg_b, ups=ups:
                                e.tensor_tensor(dst, sg_b, ups,
                                                op=AluOpType.mult), bump=dv)
                    hmul_tok[par] = (dv, htok)
                wg_free[par] = (pe, gu_tok)
            gup_dv = dv.n
            free[0] = hmul_tok[0] or free[0]
            free[1] = hmul_tok[1] or free[1]
            free[4] = sil_tok[0]
            free[5] = sil_tok[1]

            for dt in range(ND):
                par = dt % 2
                w = [wd_free[par]] if wd_free[par] else [(cc, 3)]
                for spl in range(6):
                    b0 = 8 * spl
                    b1 = min(NI, b0 + 8)
                    p.dma("sync", qd[par],
                          wdb[:, NI * 128 * par + 128 * b0:
                              NI * 128 * par + 128 * b1],
                          mkap(ccd, b0 * 262144 + dt * 16384,
                               [(128, 128), (262144, b1 - b0), (1, 128)]),
                          waits=(w if spl == 0 else []))
                wld = (qd[par], qd[par].n)
                for tcl in range(2):
                    tc = 2 * half + tcl
                    dps = ps[2 + par][:]
                    w = [wld]
                    if dt < 2 and tcl == 0:
                        w += [(dv, gup_dv), free[2 + par]]
                    if dn_free[par] is not None:
                        w.append(dn_free[par])
                    for j in range(NI):
                        tok = p.op("tensor", w if j == 0 else [],
                                   lambda e, dps=dps, par=par, j=j, tcl=tcl:
                                   e.matmul(
                                       dps,
                                       wdb[:, NI * 128 * par + 128 * j:
                                           NI * 128 * par + 128 * (j + 1)],
                                       hbuf[:, 1024 * j + 512 * tcl:
                                            1024 * j + 512 * (tcl + 1)],
                                       start=(j == 0), stop=(j == NI - 1)),
                                   bump=(pe if j == NI - 1 else None))
                    dn_tok = tok
                    wx = [dn_free[par]] if dn_free[par] is not None else []
                    if dt < 2 and tcl == 0:
                        wx.append((dv, gup_dv))
                    p.dma("gpsimd", gx[par], x2f[:, 512 * par:512 * (par + 1)],
                          x2h[128 * dt:128 * (dt + 1), 512 * tc:512 * (tc + 1)],
                          waits=wx)
                    ld = (gx[par], gx[par].n)
                    ob = outs[:, 512 * par:512 * (par + 1)]
                    wv_ = [(pe, dn_tok), ld]
                    if out_store_tok[par] is not None:
                        wv_.append(out_store_tok[par])
                    otok = p.op("vector", wv_,
                                lambda e, ob=ob, par=par, dps=dps:
                                e.tensor_tensor(
                                    ob, x2f[:, 512 * par:512 * (par + 1)], dps,
                                    op=AluOpType.add), bump=dv)
                    dn_free[par] = (dv, otok)
                    p.dma("gpsimd", gst[par],
                          out_h[128 * dt:128 * (dt + 1), 512 * tc:512 * (tc + 1)],
                          ob, waits=[(dv, otok)])
                    out_store_tok[par] = (gst[par], gst[par].n)
                wd_free[par] = (pe, dn_tok)
            free[2] = dn_free[0]
            free[3] = dn_free[1]
        if taps:
            p.dma("gpsimd", gq, hb_o[:], hbuf[:],
                  waits=[(pe, pe.n), (dv, dv.n), (ac, ac.n)])
            p.dma("gpsimd", gq, ln2b_o[:], ln2b[:])
        ph5.close()
        _finish()
    return nc


_NC_CACHE = None


def _get_nc():
    global _NC_CACHE
    if _NC_CACHE is None:
        _NC_CACHE = build_nc()
    return _NC_CACHE


def _host_constants():
    inv = 1.0 / (10000.0 ** (np.arange(0, DH, 2, dtype=np.float64) / DH))
    pos = np.arange(T, dtype=np.float64)
    ang = pos[None, :] * inv[:, None]
    cosp = np.cos(ang)
    sinp = np.sin(ang)
    cosE = np.empty((128, T), dtype=np.float32)
    sinE = np.empty((128, T), dtype=np.float32)
    cosE[0::2] = cosp
    cosE[1::2] = cosp
    sinE[0::2] = -sinp
    sinE[1::2] = sinp
    ident = np.eye(128, dtype=ml_dtypes.bfloat16)
    perm = np.zeros((128, 128), dtype=ml_dtypes.bfloat16)
    for i in range(128):
        perm[i, i ^ 1] = 1.0
    mask = np.where(np.arange(128)[None, :] <= np.arange(128)[:, None],
                    np.float32(0.0), np.float32(NEG))
    return (cosE.astype(ml_dtypes.bfloat16), sinE.astype(ml_dtypes.bfloat16),
            ident, perm, np.ascontiguousarray(mask.astype(np.float32)))


def kernel(L0_attn_x, L0_attn_Wq, L0_attn_Wk, L0_attn_Wv, L0_attn_Wo,
           L0_ffn_lut, L0_ffn_gate_idx, L0_ffn_up_idx, L0_ffn_down_idx):
    x = np.asarray(L0_attn_x, dtype=np.float32)
    wq = np.ascontiguousarray(np.asarray(L0_attn_Wq, dtype=np.float32))
    wk = np.ascontiguousarray(np.asarray(L0_attn_Wk, dtype=np.float32))
    wv = np.ascontiguousarray(np.asarray(L0_attn_Wv, dtype=np.float32))
    wo = np.ascontiguousarray(np.asarray(L0_attn_Wo, dtype=np.float32))
    lut = np.asarray(L0_ffn_lut, dtype=np.float32).reshape(16)
    gidxT = np.ascontiguousarray(np.asarray(L0_ffn_gate_idx, dtype=np.int32).T)
    uidxT = np.ascontiguousarray(np.asarray(L0_ffn_up_idx, dtype=np.int32).T)
    didxT = np.ascontiguousarray(np.asarray(L0_ffn_down_idx, dtype=np.int32).T)
    didxT_pad = np.zeros((8 * DSLOTS * 128, D), dtype=np.int32)
    didxT_pad[:INTER] = didxT

    lutrep = np.ascontiguousarray(np.broadcast_to(lut[None, :], (128, 16)))
    cosE, sinE, ident, perm, mask = _host_constants()

    nc = _get_nc()
    in_maps = []
    for c in range(8):
        in_maps.append({
            "xT": np.ascontiguousarray(x[c].T),
            "wq": wq, "wk": wk, "wv": wv, "wo": wo,
            "gsh": np.ascontiguousarray(gidxT[256 * c:256 * (c + 1)]),
            "ush": np.ascontiguousarray(uidxT[256 * c:256 * (c + 1)]),
            "dsh": np.ascontiguousarray(
                didxT_pad[DSLOTS * 128 * c:DSLOTS * 128 * (c + 1)]),
            "lutrep": lutrep,
            "cosE": cosE, "sinE": sinE,
            "identb": ident, "identf": np.eye(128, dtype=np.float32),
            "permP": perm, "maskt": mask,
        })
    res = run_bass_kernel_spmd(nc, in_maps, core_ids=list(range(8)))
    out = np.empty((B, T, D), dtype=np.float32)
    for c in range(8):
        out[c] = res.results[c]["out"].T
    return out


if __name__ == "__main__":
    build_nc()
    print("build ok")
